# version 18
# baseline (speedup 1.0000x reference)
import numpy as np
from contextlib import ExitStack
import concourse.bass as bass
import concourse.mybir as mybir
from concourse.alu_op_type import AluOpType
from concourse.bass_utils import run_bass_kernel_spmd

F32 = mybir.dt.float32
BF16 = mybir.dt.bfloat16
AF = mybir.ActivationFunctionType
AX = mybir.AxisListType
A_add = AluOpType.add
A_sub = AluOpType.subtract
A_mult = AluOpType.mult
A_min = AluOpType.min
A_byp = AluOpType.bypass

ENGS = ("te", "ve", "se", "gp", "sy")
BLK = {"te": "tensor", "ve": "vector", "se": "scalar", "gp": "gpsimd", "sy": "sync"}

C = 1024
H = 16
HD = 64
R = 8
SCALING = 2.0
DH = 4096
NCK = C // 128
HP = H // 2
EPS = 1e-5


class Sem:
    __slots__ = ("h", "n")

    def __init__(self, nc, name):
        self.h = nc.alloc_semaphore(name)
        self.n = 0


class B:

    def __init__(self, nc):
        self.nc = nc
        self.prog = {e: [] for e in ENGS}
        self.lastw = {e: {} for e in ENGS}
        self.te = Sem(nc, "p_te")
        self.ve = Sem(nc, "p_ve")
        self.se = Sem(nc, "p_se")
        self.psem = {"te": self.te, "ve": self.ve, "se": self.se}
        self.dstreams = []

    def wait(self, eng, sem, val):
        if sem is None or val is None or val <= 0:
            return
        key = id(sem)
        if self.lastw[eng].get(key, 0) >= val:
            return
        self.lastw[eng][key] = val
        self.prog[eng].append(lambda E, h=sem.h, v=val: E.wait_ge(h, v))

    def op(self, eng, fn, waits=(), mark=False):
        for wv in waits:
            if wv is None:
                continue
            self.wait(eng, wv[0], wv[1])
        if mark:
            ps = self.psem[eng]
            ps.n += 1
            self.prog[eng].append(lambda E, fn=fn, h=ps.h: fn(E).then_inc(h, 1))
            return ps.n
        self.prog[eng].append(fn)
        return None

    def dma_raw(self, q, sem, dst, src, waits=()):
        sem.n += 16
        tgt = sem.n
        for wv in waits:
            if wv is None:
                continue
            self.wait(q, wv[0], wv[1])
        self.prog[q].append(
            lambda E, d=dst, s=src, h=sem.h: E.dma_start(d, s).then_inc(h, 16))
        return (sem, tgt)

    def flush(self, block):
        for e in ENGS:
            ops = self.prog[e]
            if not ops:
                continue
            self.prog[e] = []

            def body(E, ops=ops):
                for o in ops:
                    o(E)
            getattr(block, BLK[e])(body)

    def fence_tok(self):
        toks = [(self.te, self.te.n), (self.ve, self.ve.n), (self.se, self.se.n)]
        for ds in self.dstreams:
            for s in ds.sems:
                toks.append((s, s.n))
        return [t for t in toks if t[1] > 0]

    def fence_all(self):
        toks = self.fence_tok()
        for e in ENGS:
            for (s, v) in toks:
                self.wait(e, s, v)


class DS:

    def __init__(self, b, q, name, nbuf):
        self.b = b
        self.q = q
        self.sems = [Sem(b.nc, f"ds_{name}{i}") for i in range(nbuf)]
        self.cnt = 0
        b.dstreams.append(self)

    def put(self, dst, src, waits=(), new_item=True):
        if new_item:
            self.cnt += 1
        sem = self.sems[(self.cnt - 1) % len(self.sems)]
        return self.b.dma_raw(self.q, sem, dst, src, waits)


def build_core_program(n_seq=4, T=2048):
    assert T % 128 == 0 and T >= 128
    nc = bass.Bass("TRN2", target_bir_lowering=False, debug=False)
    NT = T // 128
    TCW = min(512, T)
    NTC = T // TCW
    THALF = min(512, T)
    NHALF = T // THALF
    NTH = THALF // 128
    TCW2 = min(512, THALF)
    NTC2 = THALF // TCW2

    def din(name, shape):
        return nc.dram_tensor(name, shape, F32, kind="ExternalInput").ap()

    x_in = din("x", [n_seq, T, C])
    w_in = {k: din(k, [H, C, HD]) for k in ("Wq", "Wk", "Wv")}
    a_in = {k: din(k, [H, C, R]) for k in ("Aq", "Ak", "Av")}
    bb_in = {k: din(k, [H, R, HD]) for k in ("Bq", "Bk", "Bv")}
    wp_in = din("Wp", [C, C])
    bp_in = din("bp", [C])
    ap_in = din("Ap", [C, R])
    bpp_in = din("Bp", [R, C])
    w1_in = din("w1", [DH, C])
    w2_in = din("w2", [DH, C])
    h1_in = din("h1", [DH, DH])
    h2_in = din("h2", [DH, DH])
    w3_in = din("w3", [C, DH])
    s1_in = din("s1", [DH])
    s2_in = din("s2", [DH])
    hs1_in = din("hs1", [DH])
    hs2_in = din("hs2", [DH])
    s3_in = din("s3", [C])
    identb_in = nc.dram_tensor("identb", [128, 128], BF16, kind="ExternalInput").ap()
    identf_in = nc.dram_tensor("identf", [128, 128], F32, kind="ExternalInput").ap()
    cmask_in = nc.dram_tensor("cmask", [128, 128], F32, kind="ExternalInput").ap()

    out = nc.dram_tensor("out", [n_seq, T, C], F32, kind="ExternalOutput").ap()

    s_wqkv = {k: nc.dram_tensor(f"s_{k}", [HP, 128, NCK * 128], BF16, kind="Internal").ap()
              for k in ("Wq", "Wk", "Wv")}
    s_wp = nc.dram_tensor("s_wp", [NCK, 128, NCK * 128], BF16, kind="Internal").ap()
    s_w1 = nc.dram_tensor("s_w1", [16, 128, NCK * 256], BF16, kind="Internal").ap()
    s_w2 = nc.dram_tensor("s_w2", [16, 128, NCK * 256], BF16, kind="Internal").ap()
    s_h1 = nc.dram_tensor("s_h1", [32, 128, 32 * 128], BF16, kind="Internal").ap()
    s_h2 = nc.dram_tensor("s_h2", [32, 128, 32 * 128], BF16, kind="Internal").ap()
    s_w3 = nc.dram_tensor("s_w3", [8, 128, 32 * 128], BF16, kind="Internal").ap()
    s_y = nc.dram_tensor("s_y", [n_seq, T, C], F32, kind="Internal").ap()
    s_xn = nc.dram_tensor("s_xn", [128, NCK * 1024], BF16, kind="Internal").ap()

    b = B(nc)
    _uid = [0]

    def sbt(name, shape, dtype):
        _uid[0] += 1
        return nc.sbuf_tensor(f"{name}_{_uid[0]}", shape, dtype)

    st_const = DS(b, "sy", "const", 1)
    st_weff = DS(b, "gp", "weff", 2)
    st_wst = DS(b, "sy", "wst", 2)
    st_wpn = DS(b, "gp", "wpn", 2)
    st_apbp = DS(b, "gp", "apbp", 1)
    st_fnat = DS(b, "gp", "fnat", 2)
    st_fst = DS(b, "sy", "fst", 2)
    st_x = DS(b, "gp", "x", 2)
    st_wqkv = DS(b, "gp", "wqkv", 2)
    st_wpj = DS(b, "gp", "wpj", 2)
    st_xr = DS(b, "gp", "xr", 2)
    st_ys = DS(b, "gp", "ys", 2)
    st_sxn = DS(b, "gp", "sxn", 1)
    st_pxn = DS(b, "se", "pxn", 3)
    st_shf = DS(b, "se", "shf", 2)
    st_yb = DS(b, "se", "yb", 2)
    st_wff = DS(b, "sy", "wff", 2)
    st_xr2 = DS(b, "se", "xr2", 2)
    st_out = DS(b, "gp", "outq", 2)

    identb = nc.alloc_sbuf_tensor("identb_sb", [128, 128], BF16)
    identf = nc.alloc_sbuf_tensor("identf_sb", [128, 128], F32)
    cmask = nc.alloc_sbuf_tensor("cmask_sb", [128, 128], F32)
    s1c = nc.alloc_sbuf_tensor("s1c", [128, 32], F32)
    s2c = nc.alloc_sbuf_tensor("s2c", [128, 32], F32)
    h1c = nc.alloc_sbuf_tensor("h1c", [128, 32], F32)
    h2c = nc.alloc_sbuf_tensor("h2c", [128, 32], F32)
    w3c = nc.alloc_sbuf_tensor("w3c", [128, 8], F32)
    bpc = nc.alloc_sbuf_tensor("bpc", [128, 8], F32)
    epsc = nc.alloc_sbuf_tensor("epsc", [128, 1], F32)
    zeroc = nc.alloc_sbuf_tensor("zeroc", [128, 1], F32)
    allones = nc.alloc_sbuf_tensor("allones", [128, 1], BF16)
    onesb2 = nc.alloc_sbuf_tensor("onesb2", [128, 2], BF16)
    onesf = nc.alloc_sbuf_tensor("onesf", [128, 128], F32)
    negonesf = nc.alloc_sbuf_tensor("negonesf", [128, 128], F32)

    PSE = nc.alloc_psum_tensor("PSE", [128, 2048], F32)
    PSO = nc.alloc_psum_tensor("PSO", [128, 2048], F32)
    PSEb = PSE.bitcast(BF16)
    PSOb = PSO.bitcast(BF16)
    PS = (PSE, PSO)
    PSb = (PSEb, PSOb)

    blk = nc.Block()
    blk.__enter__()

    st_const.put(identb[:], identb_in)
    st_const.put(identf[:], identf_in)
    st_const.put(cmask[:], cmask_in)
    b.op("ve", lambda E: E.memset(epsc[:], EPS), mark=True)
    b.op("ve", lambda E: E.memset(zeroc[:], 0.0), mark=True)
    b.op("ve", lambda E: E.memset(allones[:], 1.0), mark=True)
    b.op("ve", lambda E: E.memset(onesb2[:], 0.0), mark=True)
    b.op("ve", lambda E: E.memset(onesb2[0:64, 0:1], 1.0), mark=True)
    b.op("ve", lambda E: E.memset(onesb2[64:128, 1:2], 1.0), mark=True)
    b.op("ve", lambda E: E.memset(onesf[:], 1.0), mark=True)
    b.op("ve", lambda E: E.memset(negonesf[:], -1.0), mark=True)

    with sbt("vstage", [32, 6 * 128], F32) as vstage:
        vspecs = [(s1_in, s1c, 32), (s2_in, s2c, 32), (hs1_in, h1c, 32),
                  (hs2_in, h2c, 32), (s3_in, w3c, 8), (bp_in, bpc, 8)]
        for i, (src, dst, nn) in enumerate(vspecs):
            st_const.put(vstage[0:nn, i * 128:(i + 1) * 128],
                         src.rearrange("(a p) -> a p", p=128))
        t_const = (st_const.sems[0], st_const.sems[0].n)
        t_idb = t_idf = t_msk = t_const
        vcp = 0
        for i, (src, dst, nn) in enumerate(vspecs):
            tp = b.op("te", lambda E, i=i, nn=nn: E.transpose(
                PSE[:, i * 32:i * 32 + nn],
                vstage[0:nn, i * 128:(i + 1) * 128], identf[0:nn, 0:nn]),
                waits=[t_const, (b.ve, vcp)], mark=True)
            vcp = b.op("ve", lambda E, dst=dst, i=i, nn=nn: E.tensor_copy(
                dst[:, 0:nn], PSE[:, i * 32:i * 32 + nn]),
                waits=[(b.te, tp)], mark=True)
        b.flush(blk)

    b.fence_all()
    with sbt("wnat", [128, 2, NCK * 128], BF16) as wnat, \
         sbt("anat", [128, 2, 2 * NCK * R], BF16) as anat, \
         sbt("bnat", [R, 2, 128], BF16) as bnat, \
         sbt("aT", [R, 2, 2 * NCK * 128], BF16) as aT, \
         sbt("wst", [128, 2, NCK * 128], BF16) as wst:
        te_mm = {}
        ve_stt = {}
        te_tp = {}
        ve_aT = {}
        wst_tok = {}
        for kk, (key, ka, kb2) in enumerate((("Wq", "Aq", "Bq"), ("Wk", "Ak", "Bk"),
                                             ("Wv", "Av", "Bv"))):
            wsrc, asrc, bsrc, dst = w_in[key], a_in[ka], bb_in[kb2], s_wqkv[key]
            for hp in range(HP):
                g = kk * HP + hp
                v = g % 2
                itemw = [(b.ve, ve_stt.get(g - 2, 0)), (b.te, te_tp.get(g - 2, 0)),
                         (b.te, te_mm.get(g - 2, 0))]
                tg = None
                for par in range(2):
                    h = 2 * hp + par
                    st_weff.put(
                        wnat[:, v].rearrange("p (ck q d) -> p ck q d", ck=NCK, q=2)[:, :, par],
                        wsrc[h].rearrange("(ck p) d -> p ck d", p=128),
                        waits=itemw, new_item=(par == 0))
                    st_weff.put(
                        anat[:, v].rearrange("p (q ck r) -> p q ck r", q=2, ck=NCK)[:, par],
                        asrc[h].rearrange("(ck p) r -> p ck r", p=128), new_item=False)
                    tg = st_weff.put(
                        bnat[:, v].rearrange("r (q d) -> r q d", q=2)[:, par],
                        bsrc[h], new_item=False)
                tw = [tg, tg]
                ta = [tg, tg]
                tb = [tg, tg]
                for par in range(2):
                    for ck in range(NCK):
                        j = par * NCK + ck
                        tpm = b.op("te", lambda E, v=v, j=j: E.transpose(
                            PSOb[0:R, v * 2048 + j * 128:v * 2048 + (j + 1) * 128],
                            anat[:, v, j * R:(j + 1) * R], identb[:]),
                            waits=[ta[par], t_const, (b.ve, ve_aT.get(g - 2, 0))],
                            mark=(j == 2 * NCK - 1))
                te_tp[g] = tpm
                ve_aT[g] = b.op("ve", lambda E, v=v: E.tensor_copy(
                    aT[0:R, v], PSOb[0:R, v * 2048:(v + 1) * 2048]),
                    waits=[(b.te, tpm)], mark=True)
                for par in range(2):
                    for ck in range(NCK):
                        j = par * NCK + ck
                        col = v * 1024 + ck * 128 + par * 64
                        mm = b.op("te", lambda E, j=j, col=col, v=v, par=par: E.matmul(
                            PSE[:, col:col + 64],
                            aT[0:R, v, j * 128:(j + 1) * 128],
                            bnat[0:R, v, par * 64:(par + 1) * 64],
                            start=True, stop=True),
                            waits=[(b.ve, ve_aT[g]), tb[par],
                                   (b.ve, ve_stt.get(g - 2, 0))],
                            mark=(j == 2 * NCK - 1))
                te_mm[g] = mm
                ve_stt[g] = b.op("ve", lambda E, v=v: E.scalar_tensor_tensor(
                    wst[:, v], PSE[:, v * 1024:(v + 1) * 1024], SCALING,
                    wnat[:, v], A_mult, A_add),
                    waits=[(b.te, mm), wst_tok.get(g - 2, None)] + tw, mark=True)
                wst_tok[g] = st_wst.put(dst[hp], wst[:, v], waits=[(b.ve, ve_stt[g])])

        glast = 3 * HP - 1
        for ck in range(NCK):
            st_apbp.put(anat[:, 0, ck * R:(ck + 1) * R],
                        ap_in.rearrange("(ck p) r -> p ck r", p=128)[:, ck],
                        waits=[(b.te, te_tp.get(glast, 0)), (b.te, te_mm.get(glast, 0)),
                               (b.ve, ve_stt.get(glast, 0))], new_item=(ck == 0))
        bp_tok = st_apbp.put(aT[0:R, 0, 0:C], bpp_in, new_item=False)
        ap_toks = [bp_tok] * NCK
        for ck in range(NCK):
            tpm = b.op("te", lambda E, ck=ck: E.transpose(
                PSOb[0:R, ck * 128:(ck + 1) * 128],
                anat[:, 0, ck * R:(ck + 1) * R], identb[:]),
                waits=[ap_toks[ck], (b.ve, ve_aT.get(glast, 0))],
                mark=(ck == NCK - 1))
        apT_ev = b.op("ve", lambda E: E.tensor_copy(
            aT[0:R, 1, 0:NCK * 128], PSOb[0:R, 0:NCK * 128]),
            waits=[(b.te, tpm), (b.te, te_mm.get(glast, 0))], mark=True)
        wp_mm = {}
        wpst_tok = {-1: wst_tok.get(glast, None), -2: wst_tok.get(glast - 1, None)}
        wp_stt = {glast: ve_stt.get(glast, 0)}
        for c2 in range(NCK):
            v = c2 % 2
            tw = st_wpn.put(wnat[:, v].rearrange("p (ck f) -> p ck f", ck=NCK),
                            wp_in.rearrange("(ck p) (c2 f) -> p ck c2 f", p=128, f=128)[:, :, c2],
                            waits=[(b.ve, wp_stt.get(c2 - 2, ve_stt.get(glast, 0)))])
            for ck in range(NCK):
                mm = b.op("te", lambda E, v=v, ck=ck, c2=c2: E.matmul(
                    PSE[:, v * 1024 + ck * 128:v * 1024 + (ck + 1) * 128],
                    aT[0:R, 1, ck * 128:(ck + 1) * 128],
                    aT[0:R, 0, c2 * 128:(c2 + 1) * 128],
                    start=True, stop=True),
                    waits=[(b.ve, apT_ev), bp_tok,
                           (b.ve, wp_stt.get(c2 - 2, 0))],
                    mark=(ck == NCK - 1))
            wp_mm[c2] = mm
            wp_stt[c2] = b.op("ve", lambda E, v=v: E.scalar_tensor_tensor(
                wst[:, v], PSE[:, v * 1024:(v + 1) * 1024], SCALING,
                wnat[:, v], A_mult, A_add),
                waits=[(b.te, mm), tw, wpst_tok.get(c2 - 2, wst_tok.get(glast - 1, None))],
                mark=True)
            wpst_tok[c2] = st_wst.put(s_wp[c2], wst[:, v], waits=[(b.ve, wp_stt[c2])])
        b.flush(blk)

    b.flush(blk)
    setup_toks = b.fence_tok()

    SC = [PSO[:, 0:512], PSO[:, 512:1024], PSO[:, 1024:1536]]
    SCb = [PSOb[:, 0:1024], PSOb[:, 1024:2048], PSOb[:, 2048:3072]]
    Cb_ = PSO[:, 1536:2048]
    Db_ = PSE[:, 1536:2048]
    F = [PSE[:, 0:512], PSE[:, 512:1024], PSE[:, 1024:1536]]
    FEb = [PSEb[:, 0:1024], PSEb[:, 1024:2048], PSEb[:, 2048:3072]]
    scfree = {i: list(setup_toks) for i in range(3)}
    ffree = {i: list(setup_toks) for i in range(3)}
    scn = [0]
    fn_ = [0]

    def sc_next():
        i = scn[0] % 3
        scn[0] += 1
        return i

    def f_next():
        i = fn_[0] % 3
        fn_[0] += 1
        return i

    def tm2fm_g(src_bf, dst_fm, tau, waits_src, banks, free, nxt):
        i = nxt()
        for ck in range(NCK):
            tpm = b.op("te", lambda E, ck=ck, i=i, banks=banks: E.transpose(
                banks[i][:, ck * 128:(ck + 1) * 128],
                src_bf[:, ck * 128:(ck + 1) * 128], identb[:]),
                waits=(list(waits_src) + [t_const] + free[i]) if ck == 0 else [],
                mark=(ck == NCK - 1))
        vem = b.op("ve", lambda E, i=i, tau=tau, banks=banks: E.tensor_copy(
            dst_fm.rearrange("p (ck t) -> p ck t", ck=NCK)[:, :, tau * 128:(tau + 1) * 128],
            banks[i].rearrange("p (ck j) -> p ck j", ck=NCK)),
            waits=[(b.te, tpm)], mark=True)
        free[i] = [(b.ve, vem)]
        return tpm, vem

    def tm2fm2(src_bf, dst_fm, tau, waits_src):
        return tm2fm_g(src_bf, dst_fm, tau, waits_src, SCb, scfree, sc_next)

    def tm2fmF(src_bf, dst_fm, tau, waits_src):
        return tm2fm_g(src_bf, dst_fm, tau, waits_src, FEb, ffree, f_next)

    def ln_tile(x_view, dst_view, scr, waits):
        bn, mv, st = scr
        b.op("ve", lambda E: E.bn_stats(bn[:, 0:6], x_view[:, 0:512]),
             waits=waits, mark=True)
        m2 = b.op("ve", lambda E: E.bn_stats(bn[:, 6:12], x_view[:, 512:1024]),
                  mark=True)
        m3 = b.op("ve", lambda E: E.bn_aggr(mv, bn), waits=[(b.ve, m2)], mark=True)
        m4 = b.op("se", lambda E: E.activation(st[:, 0:1], mv[:, 1:2], AF.Sqrt,
                                               bias=epsc[:], scale=1.0),
                  waits=[(b.ve, m3)], mark=True)
        m5 = b.op("ve", lambda E: E.reciprocal(st[:, 1:2], st[:, 0:1]),
                  waits=[(b.se, m4)], mark=True)
        m6 = b.op("ve", lambda E: E.tensor_scalar(
            dst_view, x_view, mv[:, 0:1], st[:, 1:2], A_sub, A_mult),
            waits=[(b.ve, m5)], mark=True)
        return m6

    outst = []
    yst_all = {}

    def gen_attn():
        with ExitStack() as _es:
            xn_fm = _es.enter_context(sbt("xn_fm", [128, NCK * 1024], BF16))
            at_fm = _es.enter_context(sbt("at_fm", [128, NCK * 1024], BF16))
            qb = _es.enter_context(sbt("qb", [128, 1024], BF16))
            kb = _es.enter_context(sbt("kb", [128, T], BF16))
            vt = _es.enter_context(sbt("vt", [128, 16 * 130], BF16))
            shtmp = _es.enter_context(sbt("shtmp", [128, 2, 512], BF16))
            vf = _es.enter_context(sbt("vf", [128, T], BF16))
            psb = _es.enter_context(sbt("psb", [128, 4, 512], BF16))
            sqb = _es.enter_context(sbt("sqb", [128, 2048], BF16))
            wqkv = _es.enter_context(sbt("wqkv", [128, 2, 1024], BF16))
            zr = _es.enter_context(sbt("zr", [128, 1024], F32))
            bcs = _es.enter_context(sbt("bcs", [128, 1024], F32))
            wpj = _es.enter_context(sbt("wpj", [128, 1024], BF16))
            xay = _es.enter_context(sbt("xay", [128, 2, 1024], F32))
            xnst = _es.enter_context(sbt("xnst", [128, 3, 1024], BF16))
            lnscr = _es.enter_context(sbt("lnscr", [128, 2, 16], F32))
            mxs = _es.enter_context(sbt("mxs", [128, 16], F32))
            khist = _es.enter_context(sbt("khist", [128, 8], F32))
            bBt = _es.enter_context(sbt("bBt", [128, 2], F32))
            psb_last = {}
            last_ve01 = {0: list(setup_toks), 1: list(setup_toks)}
            zr_last = []
            bcs_last = []
            wslot_last = {0: 0, 1: 0}
            prev_attn_te = 0
            a3_yt_end = 0
            a2_qkv_end = 0
            a3_mm_end = 0
            a1_ln_end = 0
            ys_tok = {0: None, 1: None}
            sxn_tok = None
            shf_toks = []
            tmp_last = {}
            shcnt = [0]
            vtv = vt.rearrange("p (t c) -> p t c", c=130)
            b.op("ve", lambda E: E.memset(vtv[:, :, 64:65], 1.0), mark=True)
            vones = b.op("ve", lambda E: E.memset(vtv[:, :, 129:130], 1.0), mark=True)
            for sq in range(n_seq):
                for hh2 in range(2):
                    ve_ts = {}
                    te_tp2 = {}
                    a1_ev = {}
                    lnm = {}
                    for tt in range(10):
                        if tt < 8:
                            gt = hh2 * 8 + tt
                            vv = tt % 2
                            xat = st_x.put(xay[:, vv],
                                           x_in[sq, gt * 128:(gt + 1) * 128, :],
                                           waits=[(b.ve, ve_ts.get(tt - 2, a1_ln_end)),
                                                  ys_tok[vv]])
                            m6 = ln_tile(xay[:, vv], xnst[:, tt % 3],
                                         (lnscr[:, vv, 0:12], lnscr[:, vv, 12:14],
                                          lnscr[:, vv, 14:16]),
                                         [xat, (b.te, te_tp2.get(tt - 3, 0))])
                            ve_ts[tt] = m6
                            lnm[tt] = m6
                        if tt >= 2:
                            t2 = tt - 2
                            tpm, vem = tm2fm2(xnst[:, t2 % 3], xn_fm, t2,
                                              [(b.ve, lnm[t2]), (b.te, a3_yt_end)]
                                              + ([sxn_tok] if hh2 == 1 else []))
                            te_tp2[t2] = tpm
                            a1_ev[t2] = vem
                        yield 1024
                    a1_ln_end = b.ve.n
                    if hh2 == 0:
                        sxn_tok = st_sxn.put(s_xn, xn_fm[:],
                                             waits=[(b.ve, a1_ev[7])])
                    for hp in range(HP):
                        k_tok = st_wqkv.put(wqkv[:, 0], s_wqkv["Wk"][hp],
                                            waits=[(b.te, wslot_last[0])])
                        v_tok = st_wqkv.put(wqkv[:, 1], s_wqkv["Wv"][hp],
                                            waits=[(b.te, wslot_last[1])])
                        if hh2 == 1:
                            pxt = {}
                            pmm = {}
                            for ck in range(NCK):
                                ps_ = ck % 3
                                pxt[ck] = st_pxn.put(
                                    xnst[:, ps_],
                                    s_xn[:, ck * 1024:(ck + 1) * 1024],
                                    waits=[(b.te, pmm.get(ck - 3, te_tp2[7])),
                                           sxn_tok])
                                for ri, (Xb, pvv) in enumerate(
                                        ((SC[0], None), (SC[1], None),
                                         (Cb_, None), (Db_, None))):
                                    wsl = 0 if ri < 2 else 1
                                    tcl = ri % 2
                                    ww = ([k_tok, v_tok, pxt[ck]]
                                          + scfree[0] + scfree[1]
                                          + last_ve01[0] + last_ve01[1]) \
                                        if ck == 0 else [pxt[ck]]
                                    mmp = b.op("te", lambda E, Xb=Xb, wsl=wsl, tcl=tcl, ck=ck, ps_=ps_: E.matmul(
                                        Xb,
                                        wqkv[:, wsl, ck * 128:(ck + 1) * 128],
                                        xnst[:, ps_, tcl * 512:(tcl + 1) * 512],
                                        start=(ck == 0), stop=(ck == NCK - 1)),
                                        waits=ww, mark=(ri == 3))
                                pmm[ck] = mmp
                                yield 2048
                            for ri, (Xb, dstb) in enumerate(
                                    ((SC[0], kb), (SC[1], kb), (Cb_, vf), (Db_, vf))):
                                tcl = ri % 2
                                sm = b.op("se", lambda E, Xb=Xb, dstb=dstb, tcl=tcl: E.activation(
                                    dstb[:, tcl * 512:(tcl + 1) * 512], Xb,
                                    AF.Copy, bias=0.0, scale=1.0),
                                    waits=[(b.te, pmm[NCK - 1]),
                                           (b.te, prev_attn_te)], mark=True)
                                if ri < 2:
                                    scfree[ri] = [(b.se, sm)]
                                else:
                                    last_ve01[ri - 2] = [(b.se, sm)]
                        tc0 = hh2 * 2
                        qkv_marks = {}
                        for wi, (wsl, dstb, qsc) in enumerate(
                                ((0, kb, 1.0), (1, vf, 1.0), (0, qb, 0.125))):
                            if wi == 2:
                                q_tok = st_wqkv.put(wqkv[:, 0], s_wqkv["Wq"][hp],
                                                    waits=[(b.te, qkv_marks[0])])
                            for tcl in range(2):
                                tcg = tc0 + tcl
                                i = sc_next()
                                for ck in range(NCK):
                                    mm = b.op("te", lambda E, i=i, wsl=wsl, ck=ck, tcl=tcl: E.matmul(
                                        SC[i],
                                        wqkv[:, wsl, ck * 128:(ck + 1) * 128],
                                        xn_fm[:, ck * 1024 + tcl * 512:ck * 1024 + (tcl + 1) * 512],
                                        start=(ck == 0), stop=(ck == NCK - 1)),
                                        waits=(scfree[i]
                                               + [(q_tok if wi == 2 else (k_tok if wi == 0 else v_tok)),
                                                  (b.ve, a1_ev[tcl * 4 + 3])]) 
                                        if ck == 0 else [],
                                        mark=(ck == NCK - 1))
                                dcol = (tcg if wi != 2 else tcl) * 512
                                sm = b.op("se", lambda E, i=i, dstb=dstb, dcol=dcol, qsc=qsc: E.activation(
                                    dstb[:, dcol:dcol + 512], SC[i],
                                    AF.Copy, bias=0.0, scale=qsc),
                                    waits=[(b.te, mm), (b.te, prev_attn_te)],
                                    mark=True)
                                scfree[i] = [(b.se, sm)]
                                yield 4096
                            qkv_marks[wi] = mm
                        wslot_last[0] = qkv_marks[2]
                        wslot_last[1] = qkv_marks[1]
                        qkv_evac = b.se.n
                        ntile = (hh2 + 1) * 8
                        for bt in range(ntile // 8):
                            i = sc_next()
                            for k in range(8):
                                st = bt * 8 + k
                                tpm = b.op("te", lambda E, i=i, k=k, st=st: E.transpose(
                                    SCb[i][:, k * 128:(k + 1) * 128],
                                    vf[:, st * 128:(st + 1) * 128], identb[:]),
                                    waits=(scfree[i] + [(b.se, qkv_evac)]) if k == 0 else [],
                                    mark=(k == 7))
                            srcv = SCb[i].rearrange("p (k f) -> p k f", f=128)
                            b.op("ve", lambda E, i=i, bt=bt, srcv=srcv: E.tensor_copy(
                                vtv[:, bt * 8:(bt + 1) * 8, 0:64], srcv[:, :, 0:64]),
                                waits=[(b.te, tpm), (b.te, prev_attn_te),
                                       (b.ve, vones)], mark=True)
                            vem = b.op("ve", lambda E, i=i, bt=bt, srcv=srcv: E.tensor_copy(
                                vtv[:, bt * 8:(bt + 1) * 8, 65:129], srcv[:, :, 64:128]),
                                mark=True)
                            scfree[i] = [(b.ve, vem)]
                            yield 1024
                        vt_ready = vem
                        i = sc_next()
                        sqm1 = b.op("se", lambda E: E.activation(
                            sqb[:, 0:1024], qb[:], AF.Square, bias=zeroc[:], scale=1.0),
                            waits=[(b.se, qkv_evac)], mark=True)
                        sqm2 = b.op("se", lambda E, tc0=tc0: E.activation(
                            sqb[:, 1024:2048], kb[:, tc0 * 512:(tc0 + 2) * 512],
                            AF.Square, bias=zeroc[:], scale=1.0),
                            mark=True)
                        rm = 0
                        for tc8 in range(4):
                            i2 = sc_next()
                            nm = b.op("te", lambda E, i2=i2, tc8=tc8: E.matmul(
                                SC[i2][0:2, 0:512], onesb2[:, 0:2],
                                sqb[:, tc8 * 512:(tc8 + 1) * 512],
                                start=True, stop=True),
                                waits=(scfree[i2]
                                       + [(b.se, sqm1 if tc8 < 2 else sqm2),
                                          (b.te, prev_attn_te)]),
                                mark=True)
                            rm = b.op("ve", lambda E, i2=i2, tc8=tc8: E.reduce_max(
                                mxs[0:2, tc8:tc8 + 1], SC[i2][0:2, 0:512], axis=AX.X),
                                waits=[(b.te, nm), (b.te, prev_attn_te),
                                       (b.ve, rm)], mark=True)
                            scfree[i2] = [(b.ve, rm)]
                        r1 = b.op("ve", lambda E: E.reduce_max(
                            mxs[0:2, 8:9], mxs[0:2, 0:2], axis=AX.X),
                            waits=[(b.ve, rm)], mark=True)
                        if hh2 == 0:
                            r2 = b.op("ve", lambda E: E.reduce_max(
                                mxs[0:2, 9:10], mxs[0:2, 2:4], axis=AX.X), mark=True)
                            r2 = b.op("ve", lambda E, hp=hp: E.tensor_copy(
                                khist[0:2, hp:hp + 1], mxs[0:2, 9:10]),
                                waits=[(b.ve, r2)], mark=True)
                        else:
                            r2 = b.op("ve", lambda E, hp=hp: E.tensor_copy(
                                mxs[0:2, 4:5], khist[0:2, hp:hp + 1]), mark=True)
                            r2 = b.op("ve", lambda E: E.reduce_max(
                                mxs[0:2, 9:10], mxs[0:2, 2:5], axis=AX.X),
                                waits=[(b.ve, r2)], mark=True)
                        r3 = b.op("ve", lambda E: E.tensor_tensor(
                            mxs[0:2, 10:11], mxs[0:2, 8:9], mxs[0:2, 9:10], A_mult),
                            waits=[(b.ve, r2)], mark=True)
                        tp2 = b.op("te", lambda E, i=i: E.transpose(
                            SC[i][0:1, 128:130], mxs[0:2, 10:11], identf[0:2, 0:2]),
                            waits=scfree[i] + [(b.ve, r3)], mark=True)
                        r4 = b.op("ve", lambda E, i=i: E.tensor_copy(
                            mxs[0:1, 11:13], SC[i][0:1, 128:130]),
                            waits=[(b.te, tp2)], mark=True)
                        r5 = b.op("se", lambda E: E.activation(
                            mxs[0:1, 11:13], mxs[0:1, 11:13], AF.Sqrt,
                            bias=zeroc[0:1, 0:1], scale=1.0),
                            waits=[(b.ve, r4)], mark=True)
                        bcm = {}
                        for par in range(2):
                            bcm[par] = b.op("te", lambda E, i=i, par=par: E.matmul(
                                SC[i][:, 256 + par:257 + par], negonesf[0:1, 0:128],
                                mxs[0:1, 11 + par:12 + par],
                                start=True, stop=True),
                                waits=[(b.se, r5)], mark=(par == 1))
                        bBm = b.op("se", lambda E, i=i: E.activation(
                            bBt[:, 0:2], SC[i][:, 256:258], AF.Copy, bias=0.0, scale=1.0),
                            waits=[(b.te, bcm[1])], mark=True)
                        scfree[i] = [(b.se, bBm)]
                        yield 2500
                        for tl in range(2):
                            tg = hh2 * 2 + tl
                            ns = 4 * tg + 4
                            exp_m = {}
                            av_m = {}
                            for step in range(ns + 2):
                                if step >= 2:
                                    sg = step - 2
                                    dlt = sg - 4 * tg
                                    off = 128 * dlt if dlt >= 0 else 0
                                    for par in range(2):
                                        slot = (sg % 2) * 2 + par
                                        Xb = Cb_ if par == 0 else Db_
                                        avw = [(b.se, exp_m[(sg, par)]), (b.ve, vt_ready)]
                                        if sg == 0:
                                            avw += last_ve01[par]
                                        avm = b.op("te", lambda E, Xb=Xb, sg=sg, par=par, slot=slot, off=off, ns=ns: E.matmul(
                                            Xb[0:65, off:512],
                                            vt[:, sg * 130 + par * 65:sg * 130 + par * 65 + 65],
                                            psb[:, slot, off:512],
                                            start=(sg == 0), stop=(sg == ns - 1)),
                                            waits=avw, mark=True)
                                        av_m[(sg, par)] = avm
                                        psb_last[slot] = avm
                                if step < ns:
                                    sg = step
                                    dlt = sg - 4 * tg
                                    off = 128 * dlt if dlt >= 0 else 0
                                    for par in range(2):
                                        p0 = par * 64
                                        i = sc_next()
                                        spm = b.op("te", lambda E, i=i, p0=p0, sg=sg, tl=tl, off=off: E.matmul(
                                            SC[i][:, off:512],
                                            kb[p0:p0 + 64, sg * 128:(sg + 1) * 128],
                                            qb[p0:p0 + 64, tl * 512 + off:(tl + 1) * 512],
                                            start=True, stop=True),
                                            waits=scfree[i] + [(b.se, qkv_evac)],
                                            mark=True)
                                        if dlt >= 0:
                                            mkm = b.op("ve", lambda E, i=i, off=off: E.tensor_tensor(
                                                SC[i][:, off:off + 128], SC[i][:, off:off + 128],
                                                cmask[:], A_add),
                                                waits=[(b.te, spm), t_msk], mark=True)
                                            ew = [(b.ve, mkm)]
                                        else:
                                            ew = [(b.te, spm)]
                                        slot = (sg % 2) * 2 + par
                                        em = b.op("se", lambda E, i=i, par=par, slot=slot, off=off: E.activation(
                                            psb[:, slot, off:512], SC[i][:, off:512],
                                            AF.Exp, bias=bBt[:, par:par + 1], scale=1.0),
                                            waits=ew + [(b.se, bBm),
                                                        (b.te, psb_last.get(slot, 0))],
                                            mark=True)
                                        exp_m[(sg, par)] = em
                                        scfree[i] = [(b.se, em)]
                                yield 1536
                            rc0 = b.op("ve", lambda E: E.reciprocal(
                                zr[64:65, 0:512], Cb_[64:65, 0:512]),
                                waits=[(b.te, av_m[(ns - 1, 0)])] + zr_last, mark=True)
                            rc1 = b.op("ve", lambda E: E.reciprocal(
                                zr[64:65, 512:1024], Db_[64:65, 0:512]),
                                waits=[(b.te, av_m[(ns - 1, 1)])], mark=True)
                            yield 60
                            iX = sc_next()
                            iY = sc_next()
                            bc0 = b.op("te", lambda E, iX=iX: E.matmul(
                                SC[iX][0:64, 0:512], onesf[64:65, 0:64],
                                zr[64:65, 0:512],
                                start=True, stop=True),
                                waits=scfree[iX] + [(b.ve, rc0)], mark=True)
                            bc1 = b.op("te", lambda E, iY=iY: E.matmul(
                                SC[iY][0:64, 0:512], onesf[64:65, 0:64],
                                zr[64:65, 512:1024],
                                start=True, stop=True),
                                waits=scfree[iY] + [(b.ve, rc1)], mark=True)
                            zr_last = [(b.te, bc1)]
                            sb0 = b.op("se", lambda E, iX=iX: E.activation(
                                bcs[0:64, 0:512], SC[iX][0:64, 0:512],
                                AF.Copy, bias=0.0, scale=1.0),
                                waits=[(b.te, bc0)] + bcs_last, mark=True)
                            sb1 = b.op("se", lambda E, iY=iY: E.activation(
                                bcs[0:64, 512:1024], SC[iY][0:64, 0:512],
                                AF.Copy, bias=0.0, scale=1.0),
                                waits=[(b.te, bc1)], mark=True)
                            ve0 = b.op("ve", lambda E, hp=hp, tl=tl: E.tensor_tensor(
                                at_fm[0:64, hp * 1024 + tl * 512:hp * 1024 + (tl + 1) * 512],
                                Cb_[0:64, 0:512], bcs[0:64, 0:512], A_mult),
                                waits=[(b.se, sb0), (b.te, a3_mm_end)], mark=True)
                            shs = shcnt[0] % 2
                            shcnt[0] += 1
                            ve1 = b.op("ve", lambda E, shs=shs: E.tensor_tensor(
                                shtmp[0:64, shs, :],
                                Db_[0:64, 0:512], bcs[0:64, 512:1024], A_mult),
                                waits=[(b.se, sb1), tmp_last.get(shs)], mark=True)
                            shtok = st_shf.put(
                                at_fm[64:128, hp * 1024 + tl * 512:hp * 1024 + (tl + 1) * 512],
                                shtmp[0:64, shs, :],
                                waits=[(b.ve, ve1), (b.te, a3_mm_end)])
                            shf_toks.append(shtok)
                            tmp_last[shs] = shtok
                            last_ve01[0] = [(b.ve, ve0)]
                            last_ve01[1] = [(b.ve, ve1)]
                            at_ve_last = ve0
                            bcs_last = [(b.ve, ve1)]
                            scfree[iX] = [(b.se, sb0)]
                            scfree[iY] = [(b.se, sb1)]
                            yield 1200
                        prev_attn_te = b.te.n
                    a2_qkv_end = prev_attn_te
                    pfm = xn_fm
                    at_ready = [(b.ve, at_ve_last)] + shf_toks
                    shf_toks = []
                    wp_t = {}
                    te_pmm = {}
                    for c2 in range(NCK):
                        wp_t[c2] = st_wpj.put(wpj[:], s_wp[c2],
                                              waits=[(b.te, te_pmm.get(c2 - 1, 0))])
                        for tcl in range(2):
                            i = sc_next()
                            for ck in range(NCK):
                                mm = b.op("te", lambda E, i=i, ck=ck, tcl=tcl: E.matmul(
                                    SC[i],
                                    wpj[:, ck * 128:(ck + 1) * 128],
                                    at_fm[:, ck * 1024 + tcl * 512:ck * 1024 + (tcl + 1) * 512],
                                    start=(ck == 0), stop=(ck == NCK - 1)),
                                    waits=(scfree[i] + [wp_t[c2]] + at_ready) if ck == 0 else [],
                                    mark=(ck == NCK - 1))
                            pv = b.op("ve", lambda E, i=i, c2=c2, tcl=tcl: E.tensor_scalar(
                                pfm[:, c2 * 1024 + tcl * 512:c2 * 1024 + (tcl + 1) * 512],
                                SC[i], bpc[:, c2:c2 + 1], None, A_add, A_byp),
                                waits=[(b.te, mm), (b.te, a2_qkv_end)], mark=True)
                            scfree[i] = [(b.ve, pv)]
                            yield 4096
                        te_pmm[c2] = mm
                    a3_mm_end = te_pmm[NCK - 1]
                    pev_done = b.ve.n
                    yield 60
                    ve_yt = {}
                    for tt in range(8):
                        gt = hh2 * 8 + tt
                        xrt = st_xr.put(xay[:, 0], x_in[sq, gt * 128:(gt + 1) * 128, :],
                                        waits=[(b.ve, ve_yt.get(tt - 1, a1_ln_end))])
                        i = sc_next()
                        for c2 in range(NCK):
                            tpm = b.op("te", lambda E, i=i, c2=c2, tt=tt: E.transpose(
                                SCb[i][:, c2 * 128:(c2 + 1) * 128],
                                pfm[:, c2 * 1024 + tt * 128:c2 * 1024 + (tt + 1) * 128],
                                identb[:]),
                                waits=(scfree[i] + [(b.ve, pev_done)]) if c2 == 0 else [],
                                mark=(c2 == NCK - 1))
                        ve_yt[tt] = b.op("ve", lambda E, i=i: E.tensor_tensor(
                            xay[:, 1], SCb[i], xay[:, 0], A_add),
                            waits=[(b.te, tpm), xrt, ys_tok[1]], mark=True)
                        scfree[i] = [(b.ve, ve_yt[tt])]
                        tok = st_ys.put(s_y[sq, gt * 128:(gt + 1) * 128, :],
                                        xay[:, 1], waits=[(b.ve, ve_yt[tt])])
                        yst_all[(sq, gt)] = tok
                        ys_tok[1] = tok
                        ys_tok[0] = xrt
                        yield 1024
                    a3_yt_end = b.te.n
                    a1_ln_end = b.ve.n
                yield -1

    def gen_ffn():
        with sbt("fnat", [128, 2, 2 * DH], BF16) as fnat, \
             sbt("fstage", [128, 2, 8192], BF16) as fstage:
            specs = [(w1_in, s_w1, C, 16, False), (w2_in, s_w2, C, 16, False),
                     (h1_in, s_h1, DH, 16, True), (h2_in, s_h2, DH, 16, True),
                     (w3_in, s_w3, DH, 4, True)]
            gsl = 0
            te_last = {}
            prev_ev = {}
            st_t = {}
            for (wsrc, wdst, I, NSL, split) in specs:
                NG = I // 512
                for sl in range(NSL):
                    v = gsl % 2
                    tnat = st_fnat.put(fnat[:, v, 0:2 * I].rearrange(
                        "p (oc i) -> p oc i", oc=2),
                        wsrc[sl * 256:(sl + 1) * 256, :].rearrange(
                            "(oc p) i -> p oc i", p=128),
                        waits=[(b.te, te_last.get(gsl - 2, 0))])
                    for g in range(NG):
                        pvv = g % 2
                        for k in range(8):
                            ic = g * 4 + k // 2
                            oc = k % 2
                            tpm = b.op("te", lambda E, pvv=pvv, k=k, ic=ic, oc=oc, v=v, I=I: E.transpose(
                                FEb[pvv][:, k * 128:(k + 1) * 128],
                                fnat[:, v, oc * I + ic * 128:oc * I + (ic + 1) * 128],
                                identb[:]),
                                waits=(ffree[pvv] + [tnat, t_const]) if k == 0 else [],
                                mark=(k == 7))
                        if split:
                            m0 = None
                            for oc2 in range(2):
                                m0 = b.op("ve", lambda E, pvv=pvv, v=v, g=g, oc2=oc2: E.tensor_copy(
                                    fstage[:, v, oc2 * 4096 + g * 512:oc2 * 4096 + (g + 1) * 512],
                                    FEb[pvv].rearrange("p (icl oc j) -> p icl oc j", icl=4, oc=2)[:, :, oc2]),
                                    waits=[(b.te, tpm), st_t.get(gsl - 2, None)],
                                    mark=(oc2 == 1))
                            ffree[pvv] = [(b.ve, m0)]
                        else:
                            m0 = b.op("ve", lambda E, pvv=pvv, v=v, g=g: E.tensor_copy(
                                fstage[:, v, g * 1024:(g + 1) * 1024], FEb[pvv][:, 0:1024]),
                                waits=[(b.te, tpm), st_t.get(gsl - 2, None)], mark=True)
                            ffree[pvv] = [(b.ve, m0)]
                        prev_ev[pvv] = m0
                    te_last[gsl] = tpm
                    if split:
                        st_t[gsl] = st_fst.put(
                            wdst[2 * sl:2 * sl + 2].rearrange("o p c -> p o c"),
                            fstage[:, v, 0:8192].rearrange("p (oc c) -> p oc c", oc=2),
                            waits=[(b.ve, m0)])
                    else:
                        st_t[gsl] = st_fst.put(wdst[sl], fstage[:, v, 0:NG * 1024],
                                               waits=[(b.ve, m0)])
                    gsl += 1
                    yield NG * 1024
        yield -1
        with ExitStack() as _es:
            u1 = _es.enter_context(sbt("u1", [128, 32 * 512], BF16))
            g1 = _es.enter_context(sbt("g1", [128, 32 * 512], BF16))
            xn2 = _es.enter_context(sbt("xn2", [128, NCK * 512], BF16))
            wstream = _es.enter_context(sbt("wstream", [128, 2, 4096], BF16))
            stmp = _es.enter_context(sbt("stmp", [128, 2, 512], BF16))
            ybs = _es.enter_context(sbt("ybs", [128, 2, 1024], F32))
            xnst2 = _es.enter_context(sbt("xnst2", [128, 3, 1024], BF16))
            lnscr2 = _es.enter_context(sbt("lnscr2", [128, 2, 16], F32))
            xos = _es.enter_context(sbt("xos", [128, 2, 1024], F32))
            wff_last = {}
            it2 = [0]
            quarter_te_end = 0
            b2_mm_end = 0
            out_tok = {0: None, 1: None}
            w3fm = g1

            def witem(srcs, waits0=()):
                v = it2[0] % 2
                tok = None
                for si, ssrc in enumerate(srcs):
                    tok = st_wff.put(wstream[:, v, si * 2048:si * 2048 + (4096 // len(srcs))]
                                     if len(srcs) == 2 else wstream[:, v, 0:4096],
                                     ssrc,
                                     waits=(list(waits0) + [(b.te, wff_last.get(it2[0] - 2, 0))])
                                     if si == 0 else [],
                                     new_item=(si == 0))
                it2[0] += 1
                return v, tok, it2[0] - 1

            for sq in range(n_seq):
                for hh in range(4):
                    th0 = hh * 512
                    ve_x2 = {}
                    te_x2 = {}
                    lnm2 = {}
                    for tt in range(6):
                        if tt < 4:
                            gt = hh * 4 + tt
                            v = tt % 2
                            yt = st_yb.put(ybs[:, v],
                                           s_y[sq, th0 + tt * 128:th0 + (tt + 1) * 128, :],
                                           waits=[yst_all[(sq, gt)],
                                                  (b.ve, ve_x2.get(tt - 2, 0))])
                            lnm2[tt] = ln_tile(ybs[:, v], xnst2[:, tt % 3],
                                               (lnscr2[:, v, 0:12], lnscr2[:, v, 12:14],
                                                lnscr2[:, v, 14:16]),
                                               [yt, (b.te, te_x2.get(tt - 3, 0))])
                        if tt >= 2:
                            t2 = tt - 2
                            tpm, vem = tm2fmF(xnst2[:, t2 % 3], xn2, t2,
                                              [(b.ve, lnm2[t2]), (b.te, b2_mm_end)])
                            te_x2[t2] = tpm
                            ve_x2[t2] = vem
                        yield 1024
                    xn2_done = vem
                    ve_m = {}
                    se_m = {}
                    for fs in range(16):
                        v, tok, it = witem((s_w1[fs], s_w2[fs]))
                        for sub in range(2):
                            fc = fs * 2 + sub
                            fa = f_next()
                            for ck in range(NCK):
                                mma = b.op("te", lambda E, fa=fa, v=v, sub=sub, ck=ck: E.matmul(
                                    F[fa],
                                    wstream[:, v, ck * 256 + sub * 128:ck * 256 + sub * 128 + 128],
                                    xn2[:, ck * 512:(ck + 1) * 512],
                                    start=(ck == 0), stop=(ck == NCK - 1)),
                                    waits=(ffree[fa] + [tok, (b.ve, xn2_done)])
                                    if ck == 0 else [],
                                    mark=(ck == NCK - 1))
                            fb = f_next()
                            for ck in range(NCK):
                                mmb = b.op("te", lambda E, fb=fb, v=v, sub=sub, ck=ck: E.matmul(
                                    F[fb],
                                    wstream[:, v, 2048 + ck * 256 + sub * 128:2048 + ck * 256 + sub * 128 + 128],
                                    xn2[:, ck * 512:(ck + 1) * 512],
                                    start=(ck == 0), stop=(ck == NCK - 1)),
                                    waits=(ffree[fb] + [tok]) if ck == 0 else [],
                                    mark=(ck == NCK - 1))
                            se_m[fc] = b.op("se", lambda E, fa=fa, fc=fc: E.activation(
                                stmp[:, fc % 2, :], F[fa], AF.Silu,
                                bias=zeroc[:], scale=s1c[:, fc:fc + 1]),
                                waits=[(b.te, mma), (b.ve, ve_m.get(fc - 2, 0))],
                                mark=True)
                            ffree[fa] = [(b.se, se_m[fc])]
                            ve_m[fc] = b.op("ve", lambda E, fb=fb, fc=fc: E.scalar_tensor_tensor(
                                u1[:, fc * 512:(fc + 1) * 512],
                                F[fb], s2c[:, fc:fc + 1],
                                stmp[:, fc % 2, :], A_mult, A_mult),
                                waits=[(b.te, mmb), (b.se, se_m[fc]),
                                       (b.te, quarter_te_end)], mark=True)
                            ffree[fb] = [(b.ve, ve_m[fc])]
                            yield 8192
                        wff_last[it] = mmb
                    b2_mm_end = mmb
                    b2_ve_end = b.ve.n

                    def hstage_g(h_scr, srcb, dstb, scvec, src_ready, dst_war):
                        semarks = []
                        for sl in range(32):
                            v, tok, it = witem((h_scr[sl],))
                            fa = f_next()
                            for ck in range(32):
                                mm = b.op("te", lambda E, fa=fa, v=v, ck=ck: E.matmul(
                                    F[fa],
                                    wstream[:, v, ck * 128:(ck + 1) * 128],
                                    srcb[:, ck * 512:(ck + 1) * 512],
                                    start=(ck == 0), stop=(ck == 31)),
                                    waits=(ffree[fa] + [tok] + src_ready) if ck == 0 else [],
                                    mark=(ck == 31))
                            sm = b.op("se", lambda E, fa=fa, sl=sl: E.activation(
                                dstb[:, sl * 512:(sl + 1) * 512], F[fa], AF.Gelu,
                                bias=zeroc[:], scale=scvec[:, sl:sl + 1]),
                                waits=[(b.te, mm)] + dst_war, mark=True)
                            ffree[fa] = [(b.se, sm)]
                            wff_last[it] = mm
                            semarks.append(sm)
                            yield 16384
                        return

                    yield from hstage_g(s_h1, u1, g1, h1c,
                                        [(b.ve, b2_ve_end)], [(b.te, quarter_te_end)])
                    b3_se_end = b.se.n
                    b3_mm_end = b.te.n
                    yield from hstage_g(s_h2, g1, u1, h2c,
                                        [(b.se, b3_se_end)], [(b.te, b3_mm_end)])
                    b4_se_end = b.se.n
                    b4_mm_end = b.te.n

                    w3_ve = {}
                    for sl in range(8):
                        v, tok, it = witem((s_w3[sl],))
                        fa = f_next()
                        for ck in range(32):
                            mm = b.op("te", lambda E, fa=fa, v=v, ck=ck: E.matmul(
                                F[fa],
                                wstream[:, v, ck * 128:(ck + 1) * 128],
                                u1[:, ck * 512:(ck + 1) * 512],
                                start=(ck == 0), stop=(ck == 31)),
                                waits=(ffree[fa] + [tok, (b.se, b4_se_end)])
                                if ck == 0 else [],
                                mark=(ck == 31))
                        w3_ve[sl] = b.op("ve", lambda E, fa=fa, sl=sl: E.tensor_scalar(
                            w3fm[:, sl * 512:(sl + 1) * 512], F[fa],
                            w3c[:, sl:sl + 1], None, A_mult, A_byp),
                            waits=[(b.te, mm), (b.te, b4_mm_end)], mark=True)
                        ffree[fa] = [(b.ve, w3_ve[sl])]
                        wff_last[it] = mm
                        yield 16384
                    w3_done = w3_ve[7]
                    ve_o = {}
                    for tt in range(4):
                        xrt = st_xr2.put(xos[:, 0],
                                         s_y[sq, th0 + tt * 128:th0 + (tt + 1) * 128, :],
                                         waits=[(b.ve, ve_o.get(tt - 1, 0)), out_tok[0]])
                        fa = f_next()
                        for c2 in range(NCK):
                            tpm = b.op("te", lambda E, fa=fa, c2=c2, tt=tt: E.transpose(
                                FEb[fa][:, c2 * 128:(c2 + 1) * 128],
                                w3fm[:, c2 * 512 + tt * 128:c2 * 512 + (tt + 1) * 128],
                                identb[:]),
                                waits=(ffree[fa] + [(b.ve, w3_done)]) if c2 == 0 else [],
                                mark=(c2 == NCK - 1))
                        ve_o[tt] = b.op("ve", lambda E, fa=fa: E.tensor_tensor(
                            xos[:, 1], FEb[fa], xos[:, 0], A_add),
                            waits=[(b.te, tpm), xrt, out_tok[1]], mark=True)
                        ffree[fa] = [(b.ve, ve_o[tt])]
                        tok = st_out.put(out[sq, th0 + tt * 128:th0 + (tt + 1) * 128, :],
                                         xos[:, 1], waits=[(b.ve, ve_o[tt])])
                        outst.append(tok)
                        out_tok[1] = tok
                        out_tok[0] = xrt
                        yield 1024
                    quarter_te_end = b.te.n
                yield -1

    WA = [490000, 820000, 820000, 820000][:n_seq]
    WB = [360000] + [5800000] * n_seq
    ga = gen_attn()
    gb = gen_ffn()
    a_done = [False]
    b_done = [False]

    def advance(g, done):
        try:
            return next(g)
        except StopIteration:
            done[0] = True
            return -1

    for rnd in range(n_seq + 1):
        a_alive = (rnd < n_seq) and not a_done[0]
        b_alive = not b_done[0]
        pa = 0.0
        pb = 0.0
        wa = WA[rnd] if rnd < n_seq else 1
        wb = WB[rnd]
        while a_alive or b_alive:
            if a_alive and (not b_alive or pa / wa <= pb / wb):
                v = advance(ga, a_done)
                if v < 0:
                    a_alive = False
                else:
                    pa += v
            else:
                v = advance(gb, b_done)
                if v < 0:
                    b_alive = False
                else:
                    pb += v
    b.flush(blk)

    for (s, v) in outst:
        b.wait("gp", s, v)
    b.fence_all()
    b.flush(blk)
    blk.__exit__(None, None, None)
    nc.finalize()
    return nc


def make_consts():
    import ml_dtypes
    ident_f = np.eye(128, dtype=np.float32)
    ident_b = ident_f.astype(ml_dtypes.bfloat16)
    t = np.arange(128)
    cmask = np.where(t[:, None] <= t[None, :], 0.0, -1e9).astype(np.float32)
    return ident_b, ident_f, cmask


NAME_MAP = {
    "Wq": "L0_attn_Wq", "Wk": "L0_attn_Wk", "Wv": "L0_attn_Wv",
    "Aq": "L0_attn_Aq", "Ak": "L0_attn_Ak", "Av": "L0_attn_Av",
    "Bq": "L0_attn_Bq", "Bk": "L0_attn_Bk", "Bv": "L0_attn_Bv",
    "Wp": "L0_attn_Wp", "bp": "L0_attn_bp", "Ap": "L0_attn_Ap",
    "Bp": "L0_attn_Bp",
    "w1": "L0_ffn_w1_w", "s1": "L0_ffn_w1_s",
    "w2": "L0_ffn_w2_w", "s2": "L0_ffn_w2_s",
    "h1": "L0_ffn_h1_w", "hs1": "L0_ffn_h1_s",
    "h2": "L0_ffn_h2_w", "hs2": "L0_ffn_h2_s",
    "w3": "L0_ffn_w3_w", "s3": "L0_ffn_w3_s",
}


def kernel(**inputs):
    n_cores = 8
    x = np.ascontiguousarray(inputs["L0_attn_x"], dtype=np.float32)
    ident_b, ident_f, cmask = make_consts()
    base = {k: np.ascontiguousarray(inputs[v], dtype=np.float32)
            for k, v in NAME_MAP.items()}
    base["identb"] = ident_b
    base["identf"] = ident_f
    base["cmask"] = cmask
    n_seq = x.shape[0] // n_cores
    nc = build_core_program(n_seq=n_seq, T=x.shape[1])
    in_maps = []
    for i in range(n_cores):
        m = dict(base)
        m["x"] = np.ascontiguousarray(x[i * n_seq:(i + 1) * n_seq])
        in_maps.append(m)
    res = run_bass_kernel_spmd(nc, in_maps, core_ids=list(range(n_cores)))
    return np.concatenate([r["out"] for r in res.results], axis=0)


# version 19
# speedup vs baseline: 1.0073x; 1.0073x over previous
import numpy as np
from contextlib import ExitStack
import concourse.bass as bass
import concourse.mybir as mybir
from concourse.alu_op_type import AluOpType
from concourse.bass_utils import run_bass_kernel_spmd

F32 = mybir.dt.float32
BF16 = mybir.dt.bfloat16
AF = mybir.ActivationFunctionType
AX = mybir.AxisListType
A_add = AluOpType.add
A_sub = AluOpType.subtract
A_mult = AluOpType.mult
A_min = AluOpType.min
A_byp = AluOpType.bypass

ENGS = ("te", "ve", "se", "gp", "sy")
BLK = {"te": "tensor", "ve": "vector", "se": "scalar", "gp": "gpsimd", "sy": "sync"}

C = 1024
H = 16
HD = 64
R = 8
SCALING = 2.0
DH = 4096
NCK = C // 128
HP = H // 2
EPS = 1e-5


class Sem:
    __slots__ = ("h", "n")

    def __init__(self, nc, name):
        self.h = nc.alloc_semaphore(name)
        self.n = 0


class B:

    def __init__(self, nc):
        self.nc = nc
        self.prog = {e: [] for e in ENGS}
        self.lastw = {e: {} for e in ENGS}
        self.te = Sem(nc, "p_te")
        self.ve = Sem(nc, "p_ve")
        self.se = Sem(nc, "p_se")
        self.psem = {"te": self.te, "ve": self.ve, "se": self.se}
        self.dstreams = []

    def wait(self, eng, sem, val):
        if sem is None or val is None or val <= 0:
            return
        key = id(sem)
        if self.lastw[eng].get(key, 0) >= val:
            return
        self.lastw[eng][key] = val
        self.prog[eng].append(lambda E, h=sem.h, v=val: E.wait_ge(h, v))

    def op(self, eng, fn, waits=(), mark=False):
        for wv in waits:
            if wv is None:
                continue
            self.wait(eng, wv[0], wv[1])
        if mark:
            ps = self.psem[eng]
            ps.n += 1
            self.prog[eng].append(lambda E, fn=fn, h=ps.h: fn(E).then_inc(h, 1))
            return ps.n
        self.prog[eng].append(fn)
        return None

    def dma_raw(self, q, sem, dst, src, waits=()):
        sem.n += 16
        tgt = sem.n
        for wv in waits:
            if wv is None:
                continue
            self.wait(q, wv[0], wv[1])
        self.prog[q].append(
            lambda E, d=dst, s=src, h=sem.h: E.dma_start(d, s).then_inc(h, 16))
        return (sem, tgt)

    def flush(self, block):
        for e in ENGS:
            ops = self.prog[e]
            if not ops:
                continue
            self.prog[e] = []

            def body(E, ops=ops):
                for o in ops:
                    o(E)
            getattr(block, BLK[e])(body)

    def fence_tok(self):
        toks = [(self.te, self.te.n), (self.ve, self.ve.n), (self.se, self.se.n)]
        for ds in self.dstreams:
            for s in ds.sems:
                toks.append((s, s.n))
        return [t for t in toks if t[1] > 0]

    def fence_all(self):
        toks = self.fence_tok()
        for e in ENGS:
            for (s, v) in toks:
                self.wait(e, s, v)


class DS:

    def __init__(self, b, q, name, nbuf):
        self.b = b
        self.q = q
        self.sems = [Sem(b.nc, f"ds_{name}{i}") for i in range(nbuf)]
        self.cnt = 0
        b.dstreams.append(self)

    def put(self, dst, src, waits=(), new_item=True):
        if new_item:
            self.cnt += 1
        sem = self.sems[(self.cnt - 1) % len(self.sems)]
        return self.b.dma_raw(self.q, sem, dst, src, waits)


def build_core_program(n_seq=4, T=2048):
    assert T % 128 == 0 and T >= 128
    nc = bass.Bass("TRN2", target_bir_lowering=False, debug=False)
    NT = T // 128
    TCW = min(512, T)
    NTC = T // TCW
    THALF = min(512, T)
    NHALF = T // THALF
    NTH = THALF // 128
    TCW2 = min(512, THALF)
    NTC2 = THALF // TCW2

    def din(name, shape):
        return nc.dram_tensor(name, shape, F32, kind="ExternalInput").ap()

    x_in = din("x", [n_seq, T, C])
    w_in = {k: din(k, [H, C, HD]) for k in ("Wq", "Wk", "Wv")}
    a_in = {k: din(k, [H, C, R]) for k in ("Aq", "Ak", "Av")}
    bb_in = {k: din(k, [H, R, HD]) for k in ("Bq", "Bk", "Bv")}
    wp_in = din("Wp", [C, C])
    bp_in = din("bp", [C])
    ap_in = din("Ap", [C, R])
    bpp_in = din("Bp", [R, C])
    w1_in = din("w1", [DH, C])
    w2_in = din("w2", [DH, C])
    h1_in = din("h1", [DH, DH])
    h2_in = din("h2", [DH, DH])
    w3_in = din("w3", [C, DH])
    s1_in = din("s1", [DH])
    s2_in = din("s2", [DH])
    hs1_in = din("hs1", [DH])
    hs2_in = din("hs2", [DH])
    s3_in = din("s3", [C])
    identb_in = nc.dram_tensor("identb", [128, 128], BF16, kind="ExternalInput").ap()
    identf_in = nc.dram_tensor("identf", [128, 128], F32, kind="ExternalInput").ap()
    cmask_in = nc.dram_tensor("cmask", [128, 128], F32, kind="ExternalInput").ap()

    out = nc.dram_tensor("out", [n_seq, T, C], F32, kind="ExternalOutput").ap()

    s_wqkv = {k: nc.dram_tensor(f"s_{k}", [HP, 128, NCK * 128], BF16, kind="Internal").ap()
              for k in ("Wq", "Wk", "Wv")}
    s_wp = nc.dram_tensor("s_wp", [NCK, 128, NCK * 128], BF16, kind="Internal").ap()
    s_w1 = nc.dram_tensor("s_w1", [16, 128, NCK * 256], BF16, kind="Internal").ap()
    s_w2 = nc.dram_tensor("s_w2", [16, 128, NCK * 256], BF16, kind="Internal").ap()
    s_h1 = nc.dram_tensor("s_h1", [32, 128, 32 * 128], BF16, kind="Internal").ap()
    s_h2 = nc.dram_tensor("s_h2", [32, 128, 32 * 128], BF16, kind="Internal").ap()
    s_w3 = nc.dram_tensor("s_w3", [8, 128, 32 * 128], BF16, kind="Internal").ap()
    s_y = nc.dram_tensor("s_y", [n_seq, T, C], F32, kind="Internal").ap()
    s_xn = nc.dram_tensor("s_xn", [128, NCK * 1024], BF16, kind="Internal").ap()

    b = B(nc)
    _uid = [0]

    def sbt(name, shape, dtype):
        _uid[0] += 1
        return nc.sbuf_tensor(f"{name}_{_uid[0]}", shape, dtype)

    st_const = DS(b, "sy", "const", 1)
    st_weff = DS(b, "gp", "weff", 2)
    st_wst = DS(b, "sy", "wst", 2)
    st_wpn = DS(b, "gp", "wpn", 2)
    st_apbp = DS(b, "gp", "apbp", 1)
    st_fnat = DS(b, "gp", "fnat", 2)
    st_fst = DS(b, "sy", "fst", 2)
    st_x = DS(b, "sy", "x", 2)
    st_wqkv = DS(b, "sy", "wqkv", 2)
    st_wpj = DS(b, "sy", "wpj", 2)
    st_xr = DS(b, "sy", "xr", 2)
    st_ys = DS(b, "gp", "ys", 2)
    st_sxn = DS(b, "gp", "sxn", 1)
    st_pxn = DS(b, "se", "pxn", 3)
    st_shf = DS(b, "se", "shf", 2)
    st_yb = DS(b, "se", "yb", 2)
    st_wff = DS(b, "sy", "wff", 2)
    st_xr2 = DS(b, "se", "xr2", 2)
    st_out = DS(b, "gp", "outq", 2)

    identb = nc.alloc_sbuf_tensor("identb_sb", [128, 128], BF16)
    identf = nc.alloc_sbuf_tensor("identf_sb", [128, 128], F32)
    cmask = nc.alloc_sbuf_tensor("cmask_sb", [128, 128], F32)
    s1c = nc.alloc_sbuf_tensor("s1c", [128, 32], F32)
    s2c = nc.alloc_sbuf_tensor("s2c", [128, 32], F32)
    h1c = nc.alloc_sbuf_tensor("h1c", [128, 32], F32)
    h2c = nc.alloc_sbuf_tensor("h2c", [128, 32], F32)
    w3c = nc.alloc_sbuf_tensor("w3c", [128, 8], F32)
    bpc = nc.alloc_sbuf_tensor("bpc", [128, 8], F32)
    epsc = nc.alloc_sbuf_tensor("epsc", [128, 1], F32)
    zeroc = nc.alloc_sbuf_tensor("zeroc", [128, 1], F32)
    allones = nc.alloc_sbuf_tensor("allones", [128, 1], BF16)
    onesb2 = nc.alloc_sbuf_tensor("onesb2", [128, 2], BF16)
    onesf = nc.alloc_sbuf_tensor("onesf", [128, 128], F32)
    negonesf = nc.alloc_sbuf_tensor("negonesf", [128, 128], F32)

    PSE = nc.alloc_psum_tensor("PSE", [128, 2048], F32)
    PSO = nc.alloc_psum_tensor("PSO", [128, 2048], F32)
    PSEb = PSE.bitcast(BF16)
    PSOb = PSO.bitcast(BF16)
    PS = (PSE, PSO)
    PSb = (PSEb, PSOb)

    blk = nc.Block()
    blk.__enter__()

    st_const.put(identb[:], identb_in)
    st_const.put(identf[:], identf_in)
    st_const.put(cmask[:], cmask_in)
    b.op("ve", lambda E: E.memset(epsc[:], EPS), mark=True)
    b.op("ve", lambda E: E.memset(zeroc[:], 0.0), mark=True)
    b.op("ve", lambda E: E.memset(allones[:], 1.0), mark=True)
    b.op("ve", lambda E: E.memset(onesb2[:], 0.0), mark=True)
    b.op("ve", lambda E: E.memset(onesb2[0:64, 0:1], 1.0), mark=True)
    b.op("ve", lambda E: E.memset(onesb2[64:128, 1:2], 1.0), mark=True)
    b.op("ve", lambda E: E.memset(onesf[:], 1.0), mark=True)
    b.op("ve", lambda E: E.memset(negonesf[:], -1.0), mark=True)

    with sbt("vstage", [32, 6 * 128], F32) as vstage:
        vspecs = [(s1_in, s1c, 32), (s2_in, s2c, 32), (hs1_in, h1c, 32),
                  (hs2_in, h2c, 32), (s3_in, w3c, 8), (bp_in, bpc, 8)]
        for i, (src, dst, nn) in enumerate(vspecs):
            st_const.put(vstage[0:nn, i * 128:(i + 1) * 128],
                         src.rearrange("(a p) -> a p", p=128))
        t_const = (st_const.sems[0], st_const.sems[0].n)
        t_idb = t_idf = t_msk = t_const
        vcp = 0
        for i, (src, dst, nn) in enumerate(vspecs):
            tp = b.op("te", lambda E, i=i, nn=nn: E.transpose(
                PSE[:, i * 32:i * 32 + nn],
                vstage[0:nn, i * 128:(i + 1) * 128], identf[0:nn, 0:nn]),
                waits=[t_const, (b.ve, vcp)], mark=True)
            vcp = b.op("ve", lambda E, dst=dst, i=i, nn=nn: E.tensor_copy(
                dst[:, 0:nn], PSE[:, i * 32:i * 32 + nn]),
                waits=[(b.te, tp)], mark=True)
        b.flush(blk)

    b.fence_all()
    with sbt("wnat", [128, 2, NCK * 128], BF16) as wnat, \
         sbt("anat", [128, 2, 2 * NCK * R], BF16) as anat, \
         sbt("bnat", [R, 2, 128], BF16) as bnat, \
         sbt("aT", [R, 2, 2 * NCK * 128], BF16) as aT, \
         sbt("wst", [128, 2, NCK * 128], BF16) as wst:
        te_mm = {}
        ve_stt = {}
        te_tp = {}
        ve_aT = {}
        wst_tok = {}
        for kk, (key, ka, kb2) in enumerate((("Wq", "Aq", "Bq"), ("Wk", "Ak", "Bk"),
                                             ("Wv", "Av", "Bv"))):
            wsrc, asrc, bsrc, dst = w_in[key], a_in[ka], bb_in[kb2], s_wqkv[key]
            for hp in range(HP):
                g = kk * HP + hp
                v = g % 2
                itemw = [(b.ve, ve_stt.get(g - 2, 0)), (b.te, te_tp.get(g - 2, 0)),
                         (b.te, te_mm.get(g - 2, 0))]
                tg = None
                for par in range(2):
                    h = 2 * hp + par
                    st_weff.put(
                        wnat[:, v].rearrange("p (ck q d) -> p ck q d", ck=NCK, q=2)[:, :, par],
                        wsrc[h].rearrange("(ck p) d -> p ck d", p=128),
                        waits=itemw, new_item=(par == 0))
                    st_weff.put(
                        anat[:, v].rearrange("p (q ck r) -> p q ck r", q=2, ck=NCK)[:, par],
                        asrc[h].rearrange("(ck p) r -> p ck r", p=128), new_item=False)
                    tg = st_weff.put(
                        bnat[:, v].rearrange("r (q d) -> r q d", q=2)[:, par],
                        bsrc[h], new_item=False)
                tw = [tg, tg]
                ta = [tg, tg]
                tb = [tg, tg]
                for par in range(2):
                    for ck in range(NCK):
                        j = par * NCK + ck
                        tpm = b.op("te", lambda E, v=v, j=j: E.transpose(
                            PSOb[0:R, v * 2048 + j * 128:v * 2048 + (j + 1) * 128],
                            anat[:, v, j * R:(j + 1) * R], identb[:]),
                            waits=[ta[par], t_const, (b.ve, ve_aT.get(g - 2, 0))],
                            mark=(j == 2 * NCK - 1))
                te_tp[g] = tpm
                ve_aT[g] = b.op("ve", lambda E, v=v: E.tensor_copy(
                    aT[0:R, v], PSOb[0:R, v * 2048:(v + 1) * 2048]),
                    waits=[(b.te, tpm)], mark=True)
                for par in range(2):
                    for ck in range(NCK):
                        j = par * NCK + ck
                        col = v * 1024 + ck * 128 + par * 64
                        mm = b.op("te", lambda E, j=j, col=col, v=v, par=par: E.matmul(
                            PSE[:, col:col + 64],
                            aT[0:R, v, j * 128:(j + 1) * 128],
                            bnat[0:R, v, par * 64:(par + 1) * 64],
                            start=True, stop=True),
                            waits=[(b.ve, ve_aT[g]), tb[par],
                                   (b.ve, ve_stt.get(g - 2, 0))],
                            mark=(j == 2 * NCK - 1))
                te_mm[g] = mm
                ve_stt[g] = b.op("ve", lambda E, v=v: E.scalar_tensor_tensor(
                    wst[:, v], PSE[:, v * 1024:(v + 1) * 1024], SCALING,
                    wnat[:, v], A_mult, A_add),
                    waits=[(b.te, mm), wst_tok.get(g - 2, None)] + tw, mark=True)
                wst_tok[g] = st_wst.put(dst[hp], wst[:, v], waits=[(b.ve, ve_stt[g])])

        glast = 3 * HP - 1
        for ck in range(NCK):
            st_apbp.put(anat[:, 0, ck * R:(ck + 1) * R],
                        ap_in.rearrange("(ck p) r -> p ck r", p=128)[:, ck],
                        waits=[(b.te, te_tp.get(glast, 0)), (b.te, te_mm.get(glast, 0)),
                               (b.ve, ve_stt.get(glast, 0))], new_item=(ck == 0))
        bp_tok = st_apbp.put(aT[0:R, 0, 0:C], bpp_in, new_item=False)
        ap_toks = [bp_tok] * NCK
        for ck in range(NCK):
            tpm = b.op("te", lambda E, ck=ck: E.transpose(
                PSOb[0:R, ck * 128:(ck + 1) * 128],
                anat[:, 0, ck * R:(ck + 1) * R], identb[:]),
                waits=[ap_toks[ck], (b.ve, ve_aT.get(glast, 0))],
                mark=(ck == NCK - 1))
        apT_ev = b.op("ve", lambda E: E.tensor_copy(
            aT[0:R, 1, 0:NCK * 128], PSOb[0:R, 0:NCK * 128]),
            waits=[(b.te, tpm), (b.te, te_mm.get(glast, 0))], mark=True)
        wp_mm = {}
        wpst_tok = {-1: wst_tok.get(glast, None), -2: wst_tok.get(glast - 1, None)}
        wp_stt = {glast: ve_stt.get(glast, 0)}
        for c2 in range(NCK):
            v = c2 % 2
            tw = st_wpn.put(wnat[:, v].rearrange("p (ck f) -> p ck f", ck=NCK),
                            wp_in.rearrange("(ck p) (c2 f) -> p ck c2 f", p=128, f=128)[:, :, c2],
                            waits=[(b.ve, wp_stt.get(c2 - 2, ve_stt.get(glast, 0)))])
            for ck in range(NCK):
                mm = b.op("te", lambda E, v=v, ck=ck, c2=c2: E.matmul(
                    PSE[:, v * 1024 + ck * 128:v * 1024 + (ck + 1) * 128],
                    aT[0:R, 1, ck * 128:(ck + 1) * 128],
                    aT[0:R, 0, c2 * 128:(c2 + 1) * 128],
                    start=True, stop=True),
                    waits=[(b.ve, apT_ev), bp_tok,
                           (b.ve, wp_stt.get(c2 - 2, 0))],
                    mark=(ck == NCK - 1))
            wp_mm[c2] = mm
            wp_stt[c2] = b.op("ve", lambda E, v=v: E.scalar_tensor_tensor(
                wst[:, v], PSE[:, v * 1024:(v + 1) * 1024], SCALING,
                wnat[:, v], A_mult, A_add),
                waits=[(b.te, mm), tw, wpst_tok.get(c2 - 2, wst_tok.get(glast - 1, None))],
                mark=True)
            wpst_tok[c2] = st_wst.put(s_wp[c2], wst[:, v], waits=[(b.ve, wp_stt[c2])])
        b.flush(blk)

    b.flush(blk)
    setup_toks = b.fence_tok()

    SC = [PSO[:, 0:512], PSO[:, 512:1024], PSO[:, 1024:1536]]
    SCb = [PSOb[:, 0:1024], PSOb[:, 1024:2048], PSOb[:, 2048:3072]]
    Cb_ = PSO[:, 1536:2048]
    Db_ = PSE[:, 1536:2048]
    F = [PSE[:, 0:512], PSE[:, 512:1024], PSE[:, 1024:1536]]
    FEb = [PSEb[:, 0:1024], PSEb[:, 1024:2048], PSEb[:, 2048:3072]]
    scfree = {i: list(setup_toks) for i in range(3)}
    ffree = {i: list(setup_toks) for i in range(3)}
    scn = [0]
    fn_ = [0]

    def sc_next():
        i = scn[0] % 3
        scn[0] += 1
        return i

    def f_next():
        i = fn_[0] % 3
        fn_[0] += 1
        return i

    def tm2fm_g(src_bf, dst_fm, tau, waits_src, banks, free, nxt):
        i = nxt()
        for ck in range(NCK):
            tpm = b.op("te", lambda E, ck=ck, i=i, banks=banks: E.transpose(
                banks[i][:, ck * 128:(ck + 1) * 128],
                src_bf[:, ck * 128:(ck + 1) * 128], identb[:]),
                waits=(list(waits_src) + [t_const] + free[i]) if ck == 0 else [],
                mark=(ck == NCK - 1))
        vem = b.op("ve", lambda E, i=i, tau=tau, banks=banks: E.tensor_copy(
            dst_fm.rearrange("p (ck t) -> p ck t", ck=NCK)[:, :, tau * 128:(tau + 1) * 128],
            banks[i].rearrange("p (ck j) -> p ck j", ck=NCK)),
            waits=[(b.te, tpm)], mark=True)
        free[i] = [(b.ve, vem)]
        return tpm, vem

    def tm2fm2(src_bf, dst_fm, tau, waits_src):
        return tm2fm_g(src_bf, dst_fm, tau, waits_src, SCb, scfree, sc_next)

    def tm2fmF(src_bf, dst_fm, tau, waits_src):
        return tm2fm_g(src_bf, dst_fm, tau, waits_src, FEb, ffree, f_next)

    def ln_tile(x_view, dst_view, scr, waits):
        bn, mv, st = scr
        b.op("ve", lambda E: E.bn_stats(bn[:, 0:6], x_view[:, 0:512]),
             waits=waits, mark=True)
        m2 = b.op("ve", lambda E: E.bn_stats(bn[:, 6:12], x_view[:, 512:1024]),
                  mark=True)
        m3 = b.op("ve", lambda E: E.bn_aggr(mv, bn), waits=[(b.ve, m2)], mark=True)
        m4 = b.op("se", lambda E: E.activation(st[:, 0:1], mv[:, 1:2], AF.Sqrt,
                                               bias=epsc[:], scale=1.0),
                  waits=[(b.ve, m3)], mark=True)
        m5 = b.op("ve", lambda E: E.reciprocal(st[:, 1:2], st[:, 0:1]),
                  waits=[(b.se, m4)], mark=True)
        m6 = b.op("ve", lambda E: E.tensor_scalar(
            dst_view, x_view, mv[:, 0:1], st[:, 1:2], A_sub, A_mult),
            waits=[(b.ve, m5)], mark=True)
        return m6

    outst = []
    yst_all = {}

    def gen_attn():
        with ExitStack() as _es:
            xn_fm = _es.enter_context(sbt("xn_fm", [128, NCK * 1024], BF16))
            at_fm = _es.enter_context(sbt("at_fm", [128, NCK * 1024], BF16))
            qb = _es.enter_context(sbt("qb", [128, 1024], BF16))
            kb = _es.enter_context(sbt("kb", [128, T], BF16))
            vt = _es.enter_context(sbt("vt", [128, 16 * 130], BF16))
            shtmp = _es.enter_context(sbt("shtmp", [128, 2, 512], BF16))
            vf = _es.enter_context(sbt("vf", [128, T], BF16))
            psb = _es.enter_context(sbt("psb", [128, 6, 512], BF16))
            sqb = _es.enter_context(sbt("sqb", [128, 2048], BF16))
            wqkv = _es.enter_context(sbt("wqkv", [128, 2, 1024], BF16))
            zr = _es.enter_context(sbt("zr", [128, 1024], F32))
            bcs = _es.enter_context(sbt("bcs", [128, 1024], F32))
            wpj = _es.enter_context(sbt("wpj", [128, 1024], BF16))
            xay = _es.enter_context(sbt("xay", [128, 2, 1024], F32))
            xnst = _es.enter_context(sbt("xnst", [128, 3, 1024], BF16))
            lnscr = _es.enter_context(sbt("lnscr", [128, 2, 16], F32))
            mxs = _es.enter_context(sbt("mxs", [128, 16], F32))
            khist = _es.enter_context(sbt("khist", [128, 8], F32))
            bBt = _es.enter_context(sbt("bBt", [128, 2], F32))
            psb_last = {}
            last_ve01 = {0: list(setup_toks), 1: list(setup_toks)}
            zr_last = []
            bcs_last = []
            wslot_last = {0: 0, 1: 0}
            prev_attn_te = 0
            a3_yt_end = 0
            a2_qkv_end = 0
            a3_mm_end = 0
            a1_ln_end = 0
            ys_tok = {0: None, 1: None}
            sxn_tok = None
            shf_toks = []
            tmp_last = {}
            shcnt = [0]
            vtv = vt.rearrange("p (t c) -> p t c", c=130)
            b.op("ve", lambda E: E.memset(vtv[:, :, 64:65], 1.0), mark=True)
            vones = b.op("ve", lambda E: E.memset(vtv[:, :, 129:130], 1.0), mark=True)
            for sq in range(n_seq):
                for hh2 in range(2):
                    ve_ts = {}
                    te_tp2 = {}
                    a1_ev = {}
                    lnm = {}
                    for tt in range(10):
                        if tt < 8:
                            gt = hh2 * 8 + tt
                            vv = tt % 2
                            xat = st_x.put(xay[:, vv],
                                           x_in[sq, gt * 128:(gt + 1) * 128, :],
                                           waits=[(b.ve, ve_ts.get(tt - 2, a1_ln_end)),
                                                  ys_tok[vv]])
                            m6 = ln_tile(xay[:, vv], xnst[:, tt % 3],
                                         (lnscr[:, vv, 0:12], lnscr[:, vv, 12:14],
                                          lnscr[:, vv, 14:16]),
                                         [xat, (b.te, te_tp2.get(tt - 3, 0))])
                            ve_ts[tt] = m6
                            lnm[tt] = m6
                        if tt >= 2:
                            t2 = tt - 2
                            tpm, vem = tm2fm2(xnst[:, t2 % 3], xn_fm, t2,
                                              [(b.ve, lnm[t2]), (b.te, a3_yt_end)]
                                              + ([sxn_tok] if hh2 == 1 else []))
                            te_tp2[t2] = tpm
                            a1_ev[t2] = vem
                        yield 1024
                    a1_ln_end = b.ve.n
                    if hh2 == 0:
                        sxn_tok = st_sxn.put(s_xn, xn_fm[:],
                                             waits=[(b.ve, a1_ev[7])])
                    for hp in range(HP):
                        k_tok = st_wqkv.put(wqkv[:, 0], s_wqkv["Wk"][hp],
                                            waits=[(b.te, wslot_last[0])])
                        v_tok = st_wqkv.put(wqkv[:, 1], s_wqkv["Wv"][hp],
                                            waits=[(b.te, wslot_last[1])])
                        if hh2 == 1:
                            pxt = {}
                            pmm = {}
                            for ck in range(NCK):
                                ps_ = ck % 3
                                pxt[ck] = st_pxn.put(
                                    xnst[:, ps_],
                                    s_xn[:, ck * 1024:(ck + 1) * 1024],
                                    waits=[(b.te, pmm.get(ck - 3, te_tp2[7])),
                                           sxn_tok])
                                for ri, (Xb, pvv) in enumerate(
                                        ((SC[0], None), (SC[1], None),
                                         (Cb_, None), (Db_, None))):
                                    wsl = 0 if ri < 2 else 1
                                    tcl = ri % 2
                                    ww = ([k_tok, v_tok, pxt[ck]]
                                          + scfree[0] + scfree[1]
                                          + last_ve01[0] + last_ve01[1]) \
                                        if ck == 0 else [pxt[ck]]
                                    mmp = b.op("te", lambda E, Xb=Xb, wsl=wsl, tcl=tcl, ck=ck, ps_=ps_: E.matmul(
                                        Xb,
                                        wqkv[:, wsl, ck * 128:(ck + 1) * 128],
                                        xnst[:, ps_, tcl * 512:(tcl + 1) * 512],
                                        start=(ck == 0), stop=(ck == NCK - 1)),
                                        waits=ww, mark=(ri == 3))
                                pmm[ck] = mmp
                                yield 2048
                            for ri, (Xb, dstb) in enumerate(
                                    ((SC[0], kb), (SC[1], kb), (Cb_, vf), (Db_, vf))):
                                tcl = ri % 2
                                sm = b.op("se", lambda E, Xb=Xb, dstb=dstb, tcl=tcl: E.activation(
                                    dstb[:, tcl * 512:(tcl + 1) * 512], Xb,
                                    AF.Copy, bias=0.0, scale=1.0),
                                    waits=[(b.te, pmm[NCK - 1]),
                                           (b.te, prev_attn_te)], mark=True)
                                if ri < 2:
                                    scfree[ri] = [(b.se, sm)]
                                else:
                                    last_ve01[ri - 2] = [(b.se, sm)]
                        tc0 = hh2 * 2
                        qkv_marks = {}
                        for wi, (wsl, dstb, qsc) in enumerate(
                                ((0, kb, 1.0), (1, vf, 1.0), (0, qb, 0.125))):
                            if wi == 2:
                                q_tok = st_wqkv.put(wqkv[:, 0], s_wqkv["Wq"][hp],
                                                    waits=[(b.te, qkv_marks[0])])
                            for tcl in range(2):
                                tcg = tc0 + tcl
                                i = sc_next()
                                for ck in range(NCK):
                                    mm = b.op("te", lambda E, i=i, wsl=wsl, ck=ck, tcl=tcl: E.matmul(
                                        SC[i],
                                        wqkv[:, wsl, ck * 128:(ck + 1) * 128],
                                        xn_fm[:, ck * 1024 + tcl * 512:ck * 1024 + (tcl + 1) * 512],
                                        start=(ck == 0), stop=(ck == NCK - 1)),
                                        waits=(scfree[i]
                                               + [(q_tok if wi == 2 else (k_tok if wi == 0 else v_tok)),
                                                  (b.ve, a1_ev[tcl * 4 + 3])]) 
                                        if ck == 0 else [],
                                        mark=(ck == NCK - 1))
                                dcol = (tcg if wi != 2 else tcl) * 512
                                sm = b.op("se", lambda E, i=i, dstb=dstb, dcol=dcol, qsc=qsc: E.activation(
                                    dstb[:, dcol:dcol + 512], SC[i],
                                    AF.Copy, bias=0.0, scale=qsc),
                                    waits=[(b.te, mm), (b.te, prev_attn_te)],
                                    mark=True)
                                scfree[i] = [(b.se, sm)]
                                yield 4096
                            qkv_marks[wi] = mm
                        wslot_last[0] = qkv_marks[2]
                        wslot_last[1] = qkv_marks[1]
                        qkv_evac = b.se.n
                        ntile = (hh2 + 1) * 8
                        for bt in range(ntile // 8):
                            i = sc_next()
                            for k in range(8):
                                st = bt * 8 + k
                                tpm = b.op("te", lambda E, i=i, k=k, st=st: E.transpose(
                                    SCb[i][:, k * 128:(k + 1) * 128],
                                    vf[:, st * 128:(st + 1) * 128], identb[:]),
                                    waits=(scfree[i] + [(b.se, qkv_evac)]) if k == 0 else [],
                                    mark=(k == 7))
                            srcv = SCb[i].rearrange("p (k f) -> p k f", f=128)
                            b.op("ve", lambda E, i=i, bt=bt, srcv=srcv: E.tensor_copy(
                                vtv[:, bt * 8:(bt + 1) * 8, 0:64], srcv[:, :, 0:64]),
                                waits=[(b.te, tpm), (b.te, prev_attn_te),
                                       (b.ve, vones)], mark=True)
                            vem = b.op("ve", lambda E, i=i, bt=bt, srcv=srcv: E.tensor_copy(
                                vtv[:, bt * 8:(bt + 1) * 8, 65:129], srcv[:, :, 64:128]),
                                mark=True)
                            scfree[i] = [(b.ve, vem)]
                            yield 1024
                        vt_ready = vem
                        i = sc_next()
                        sqm1 = b.op("se", lambda E: E.activation(
                            sqb[:, 0:1024], qb[:], AF.Square, bias=zeroc[:], scale=1.0),
                            waits=[(b.se, qkv_evac)], mark=True)
                        sqm2 = b.op("se", lambda E, tc0=tc0: E.activation(
                            sqb[:, 1024:2048], kb[:, tc0 * 512:(tc0 + 2) * 512],
                            AF.Square, bias=zeroc[:], scale=1.0),
                            mark=True)
                        rm = 0
                        for tc8 in range(4):
                            i2 = sc_next()
                            nm = b.op("te", lambda E, i2=i2, tc8=tc8: E.matmul(
                                SC[i2][0:2, 0:512], onesb2[:, 0:2],
                                sqb[:, tc8 * 512:(tc8 + 1) * 512],
                                start=True, stop=True),
                                waits=(scfree[i2]
                                       + [(b.se, sqm1 if tc8 < 2 else sqm2),
                                          (b.te, prev_attn_te)]),
                                mark=True)
                            rm = b.op("ve", lambda E, i2=i2, tc8=tc8: E.reduce_max(
                                mxs[0:2, tc8:tc8 + 1], SC[i2][0:2, 0:512], axis=AX.X),
                                waits=[(b.te, nm), (b.te, prev_attn_te),
                                       (b.ve, rm)], mark=True)
                            scfree[i2] = [(b.ve, rm)]
                        r1 = b.op("ve", lambda E: E.reduce_max(
                            mxs[0:2, 8:9], mxs[0:2, 0:2], axis=AX.X),
                            waits=[(b.ve, rm)], mark=True)
                        if hh2 == 0:
                            r2 = b.op("ve", lambda E: E.reduce_max(
                                mxs[0:2, 9:10], mxs[0:2, 2:4], axis=AX.X), mark=True)
                            r2 = b.op("ve", lambda E, hp=hp: E.tensor_copy(
                                khist[0:2, hp:hp + 1], mxs[0:2, 9:10]),
                                waits=[(b.ve, r2)], mark=True)
                        else:
                            r2 = b.op("ve", lambda E, hp=hp: E.tensor_copy(
                                mxs[0:2, 4:5], khist[0:2, hp:hp + 1]), mark=True)
                            r2 = b.op("ve", lambda E: E.reduce_max(
                                mxs[0:2, 9:10], mxs[0:2, 2:5], axis=AX.X),
                                waits=[(b.ve, r2)], mark=True)
                        r3 = b.op("ve", lambda E: E.tensor_tensor(
                            mxs[0:2, 10:11], mxs[0:2, 8:9], mxs[0:2, 9:10], A_mult),
                            waits=[(b.ve, r2)], mark=True)
                        tp2 = b.op("te", lambda E, i=i: E.transpose(
                            SC[i][0:1, 128:130], mxs[0:2, 10:11], identf[0:2, 0:2]),
                            waits=scfree[i] + [(b.ve, r3)], mark=True)
                        r4 = b.op("ve", lambda E, i=i: E.tensor_copy(
                            mxs[0:1, 11:13], SC[i][0:1, 128:130]),
                            waits=[(b.te, tp2)], mark=True)
                        r5 = b.op("se", lambda E: E.activation(
                            mxs[0:1, 11:13], mxs[0:1, 11:13], AF.Sqrt,
                            bias=zeroc[0:1, 0:1], scale=1.0),
                            waits=[(b.ve, r4)], mark=True)
                        bcm = {}
                        for par in range(2):
                            bcm[par] = b.op("te", lambda E, i=i, par=par: E.matmul(
                                SC[i][:, 256 + par:257 + par], negonesf[0:1, 0:128],
                                mxs[0:1, 11 + par:12 + par],
                                start=True, stop=True),
                                waits=[(b.se, r5)], mark=(par == 1))
                        bBm = b.op("se", lambda E, i=i: E.activation(
                            bBt[:, 0:2], SC[i][:, 256:258], AF.Copy, bias=0.0, scale=1.0),
                            waits=[(b.te, bcm[1])], mark=True)
                        scfree[i] = [(b.se, bBm)]
                        yield 2500
                        for tl in range(2):
                            tg = hh2 * 2 + tl
                            ns = 4 * tg + 4
                            exp_m = {}
                            av_m = {}
                            for step in range(ns + 3):
                                if step >= 3:
                                    sg = step - 3
                                    dlt = sg - 4 * tg
                                    off = 128 * dlt if dlt >= 0 else 0
                                    for par in range(2):
                                        slot = (sg % 3) * 2 + par
                                        Xb = Cb_ if par == 0 else Db_
                                        avw = [(b.se, exp_m[(sg, par)]), (b.ve, vt_ready)]
                                        if sg == 0:
                                            avw += last_ve01[par]
                                        avm = b.op("te", lambda E, Xb=Xb, sg=sg, par=par, slot=slot, off=off, ns=ns: E.matmul(
                                            Xb[0:65, off:512],
                                            vt[:, sg * 130 + par * 65:sg * 130 + par * 65 + 65],
                                            psb[:, slot, off:512],
                                            start=(sg == 0), stop=(sg == ns - 1)),
                                            waits=avw, mark=True)
                                        av_m[(sg, par)] = avm
                                        psb_last[slot] = avm
                                if step < ns:
                                    sg = step
                                    dlt = sg - 4 * tg
                                    off = 128 * dlt if dlt >= 0 else 0
                                    for par in range(2):
                                        p0 = par * 64
                                        i = sc_next()
                                        spm = b.op("te", lambda E, i=i, p0=p0, sg=sg, tl=tl, off=off: E.matmul(
                                            SC[i][:, off:512],
                                            kb[p0:p0 + 64, sg * 128:(sg + 1) * 128],
                                            qb[p0:p0 + 64, tl * 512 + off:(tl + 1) * 512],
                                            start=True, stop=True),
                                            waits=scfree[i] + [(b.se, qkv_evac)],
                                            mark=True)
                                        if dlt >= 0:
                                            mkm = b.op("ve", lambda E, i=i, off=off: E.tensor_tensor(
                                                SC[i][:, off:off + 128], SC[i][:, off:off + 128],
                                                cmask[:], A_add),
                                                waits=[(b.te, spm), t_msk], mark=True)
                                            ew = [(b.ve, mkm)]
                                        else:
                                            ew = [(b.te, spm)]
                                        slot = (sg % 3) * 2 + par
                                        em = b.op("se", lambda E, i=i, par=par, slot=slot, off=off: E.activation(
                                            psb[:, slot, off:512], SC[i][:, off:512],
                                            AF.Exp, bias=bBt[:, par:par + 1], scale=1.0),
                                            waits=ew + [(b.se, bBm),
                                                        (b.te, psb_last.get(slot, 0))],
                                            mark=True)
                                        exp_m[(sg, par)] = em
                                        scfree[i] = [(b.se, em)]
                                yield 1536
                            rc0 = b.op("ve", lambda E: E.reciprocal(
                                zr[64:65, 0:512], Cb_[64:65, 0:512]),
                                waits=[(b.te, av_m[(ns - 1, 0)])] + zr_last, mark=True)
                            rc1 = b.op("ve", lambda E: E.reciprocal(
                                zr[64:65, 512:1024], Db_[64:65, 0:512]),
                                waits=[(b.te, av_m[(ns - 1, 1)])], mark=True)
                            yield 60
                            iX = sc_next()
                            iY = sc_next()
                            bc0 = b.op("te", lambda E, iX=iX: E.matmul(
                                SC[iX][0:64, 0:512], onesf[64:65, 0:64],
                                zr[64:65, 0:512],
                                start=True, stop=True),
                                waits=scfree[iX] + [(b.ve, rc0)], mark=True)
                            bc1 = b.op("te", lambda E, iY=iY: E.matmul(
                                SC[iY][0:64, 0:512], onesf[64:65, 0:64],
                                zr[64:65, 512:1024],
                                start=True, stop=True),
                                waits=scfree[iY] + [(b.ve, rc1)], mark=True)
                            zr_last = [(b.te, bc1)]
                            sb0 = b.op("se", lambda E, iX=iX: E.activation(
                                bcs[0:64, 0:512], SC[iX][0:64, 0:512],
                                AF.Copy, bias=0.0, scale=1.0),
                                waits=[(b.te, bc0)] + bcs_last, mark=True)
                            sb1 = b.op("se", lambda E, iY=iY: E.activation(
                                bcs[0:64, 512:1024], SC[iY][0:64, 0:512],
                                AF.Copy, bias=0.0, scale=1.0),
                                waits=[(b.te, bc1)], mark=True)
                            ve0 = b.op("ve", lambda E, hp=hp, tl=tl: E.tensor_tensor(
                                at_fm[0:64, hp * 1024 + tl * 512:hp * 1024 + (tl + 1) * 512],
                                Cb_[0:64, 0:512], bcs[0:64, 0:512], A_mult),
                                waits=[(b.se, sb0), (b.te, a3_mm_end)], mark=True)
                            shs = shcnt[0] % 2
                            shcnt[0] += 1
                            ve1 = b.op("ve", lambda E, shs=shs: E.tensor_tensor(
                                shtmp[0:64, shs, :],
                                Db_[0:64, 0:512], bcs[0:64, 512:1024], A_mult),
                                waits=[(b.se, sb1), tmp_last.get(shs)], mark=True)
                            shtok = st_shf.put(
                                at_fm[64:128, hp * 1024 + tl * 512:hp * 1024 + (tl + 1) * 512],
                                shtmp[0:64, shs, :],
                                waits=[(b.ve, ve1), (b.te, a3_mm_end)])
                            shf_toks.append(shtok)
                            tmp_last[shs] = shtok
                            last_ve01[0] = [(b.ve, ve0)]
                            last_ve01[1] = [(b.ve, ve1)]
                            at_ve_last = ve0
                            bcs_last = [(b.ve, ve1)]
                            scfree[iX] = [(b.se, sb0)]
                            scfree[iY] = [(b.se, sb1)]
                            yield 1200
                        prev_attn_te = b.te.n
                    a2_qkv_end = prev_attn_te
                    pfm = xn_fm
                    at_ready = [(b.ve, at_ve_last)] + shf_toks
                    shf_toks = []
                    wp_t = {}
                    te_pmm = {}
                    for c2 in range(NCK):
                        wp_t[c2] = st_wpj.put(wpj[:], s_wp[c2],
                                              waits=[(b.te, te_pmm.get(c2 - 1, 0))])
                        for tcl in range(2):
                            i = sc_next()
                            for ck in range(NCK):
                                mm = b.op("te", lambda E, i=i, ck=ck, tcl=tcl: E.matmul(
                                    SC[i],
                                    wpj[:, ck * 128:(ck + 1) * 128],
                                    at_fm[:, ck * 1024 + tcl * 512:ck * 1024 + (tcl + 1) * 512],
                                    start=(ck == 0), stop=(ck == NCK - 1)),
                                    waits=(scfree[i] + [wp_t[c2]] + at_ready) if ck == 0 else [],
                                    mark=(ck == NCK - 1))
                            pv = b.op("ve", lambda E, i=i, c2=c2, tcl=tcl: E.tensor_scalar(
                                pfm[:, c2 * 1024 + tcl * 512:c2 * 1024 + (tcl + 1) * 512],
                                SC[i], bpc[:, c2:c2 + 1], None, A_add, A_byp),
                                waits=[(b.te, mm), (b.te, a2_qkv_end)], mark=True)
                            scfree[i] = [(b.ve, pv)]
                            yield 4096
                        te_pmm[c2] = mm
                    a3_mm_end = te_pmm[NCK - 1]
                    pev_done = b.ve.n
                    yield 60
                    ve_yt = {}
                    for tt in range(8):
                        gt = hh2 * 8 + tt
                        xrt = st_xr.put(xay[:, 0], x_in[sq, gt * 128:(gt + 1) * 128, :],
                                        waits=[(b.ve, ve_yt.get(tt - 1, a1_ln_end))])
                        i = sc_next()
                        for c2 in range(NCK):
                            tpm = b.op("te", lambda E, i=i, c2=c2, tt=tt: E.transpose(
                                SCb[i][:, c2 * 128:(c2 + 1) * 128],
                                pfm[:, c2 * 1024 + tt * 128:c2 * 1024 + (tt + 1) * 128],
                                identb[:]),
                                waits=(scfree[i] + [(b.ve, pev_done)]) if c2 == 0 else [],
                                mark=(c2 == NCK - 1))
                        ve_yt[tt] = b.op("ve", lambda E, i=i: E.tensor_tensor(
                            xay[:, 1], SCb[i], xay[:, 0], A_add),
                            waits=[(b.te, tpm), xrt, ys_tok[1]], mark=True)
                        scfree[i] = [(b.ve, ve_yt[tt])]
                        tok = st_ys.put(s_y[sq, gt * 128:(gt + 1) * 128, :],
                                        xay[:, 1], waits=[(b.ve, ve_yt[tt])])
                        yst_all[(sq, gt)] = tok
                        ys_tok[1] = tok
                        ys_tok[0] = xrt
                        yield 1024
                    a3_yt_end = b.te.n
                    a1_ln_end = b.ve.n
                yield -1

    def gen_ffn():
        with sbt("fnat", [128, 2, 2 * DH], BF16) as fnat, \
             sbt("fstage", [128, 2, 8192], BF16) as fstage:
            specs = [(w1_in, s_w1, C, 16, False), (w2_in, s_w2, C, 16, False),
                     (h1_in, s_h1, DH, 16, True), (h2_in, s_h2, DH, 16, True),
                     (w3_in, s_w3, DH, 4, True)]
            gsl = 0
            te_last = {}
            prev_ev = {}
            st_t = {}
            for (wsrc, wdst, I, NSL, split) in specs:
                NG = I // 512
                for sl in range(NSL):
                    v = gsl % 2
                    tnat = st_fnat.put(fnat[:, v, 0:2 * I].rearrange(
                        "p (oc i) -> p oc i", oc=2),
                        wsrc[sl * 256:(sl + 1) * 256, :].rearrange(
                            "(oc p) i -> p oc i", p=128),
                        waits=[(b.te, te_last.get(gsl - 2, 0))])
                    for g in range(NG):
                        pvv = g % 2
                        for k in range(8):
                            ic = g * 4 + k // 2
                            oc = k % 2
                            tpm = b.op("te", lambda E, pvv=pvv, k=k, ic=ic, oc=oc, v=v, I=I: E.transpose(
                                FEb[pvv][:, k * 128:(k + 1) * 128],
                                fnat[:, v, oc * I + ic * 128:oc * I + (ic + 1) * 128],
                                identb[:]),
                                waits=(ffree[pvv] + [tnat, t_const]) if k == 0 else [],
                                mark=(k == 7))
                        if split:
                            m0 = None
                            for oc2 in range(2):
                                m0 = b.op("ve", lambda E, pvv=pvv, v=v, g=g, oc2=oc2: E.tensor_copy(
                                    fstage[:, v, oc2 * 4096 + g * 512:oc2 * 4096 + (g + 1) * 512],
                                    FEb[pvv].rearrange("p (icl oc j) -> p icl oc j", icl=4, oc=2)[:, :, oc2]),
                                    waits=[(b.te, tpm), st_t.get(gsl - 2, None)],
                                    mark=(oc2 == 1))
                            ffree[pvv] = [(b.ve, m0)]
                        else:
                            m0 = b.op("ve", lambda E, pvv=pvv, v=v, g=g: E.tensor_copy(
                                fstage[:, v, g * 1024:(g + 1) * 1024], FEb[pvv][:, 0:1024]),
                                waits=[(b.te, tpm), st_t.get(gsl - 2, None)], mark=True)
                            ffree[pvv] = [(b.ve, m0)]
                        prev_ev[pvv] = m0
                    te_last[gsl] = tpm
                    if split:
                        st_t[gsl] = st_fst.put(
                            wdst[2 * sl:2 * sl + 2].rearrange("o p c -> p o c"),
                            fstage[:, v, 0:8192].rearrange("p (oc c) -> p oc c", oc=2),
                            waits=[(b.ve, m0)])
                    else:
                        st_t[gsl] = st_fst.put(wdst[sl], fstage[:, v, 0:NG * 1024],
                                               waits=[(b.ve, m0)])
                    gsl += 1
                    yield NG * 1024
        yield -1
        with ExitStack() as _es:
            u1 = _es.enter_context(sbt("u1", [128, 32 * 512], BF16))
            g1 = _es.enter_context(sbt("g1", [128, 32 * 512], BF16))
            xn2 = _es.enter_context(sbt("xn2", [128, NCK * 512], BF16))
            wstream = _es.enter_context(sbt("wstream", [128, 2, 4096], BF16))
            stmp = _es.enter_context(sbt("stmp", [128, 2, 512], BF16))
            ybs = _es.enter_context(sbt("ybs", [128, 2, 1024], F32))
            xnst2 = _es.enter_context(sbt("xnst2", [128, 3, 1024], BF16))
            lnscr2 = _es.enter_context(sbt("lnscr2", [128, 2, 16], F32))
            xos = _es.enter_context(sbt("xos", [128, 2, 1024], F32))
            wff_last = {}
            it2 = [0]
            quarter_te_end = 0
            b2_mm_end = 0
            out_tok = {0: None, 1: None}
            w3fm = g1

            def witem(srcs, waits0=()):
                v = it2[0] % 2
                tok = None
                for si, ssrc in enumerate(srcs):
                    tok = st_wff.put(wstream[:, v, si * 2048:si * 2048 + (4096 // len(srcs))]
                                     if len(srcs) == 2 else wstream[:, v, 0:4096],
                                     ssrc,
                                     waits=(list(waits0) + [(b.te, wff_last.get(it2[0] - 2, 0))])
                                     if si == 0 else [],
                                     new_item=(si == 0))
                it2[0] += 1
                return v, tok, it2[0] - 1

            for sq in range(n_seq):
                for hh in range(4):
                    th0 = hh * 512
                    ve_x2 = {}
                    te_x2 = {}
                    lnm2 = {}
                    for tt in range(6):
                        if tt < 4:
                            gt = hh * 4 + tt
                            v = tt % 2
                            yt = st_yb.put(ybs[:, v],
                                           s_y[sq, th0 + tt * 128:th0 + (tt + 1) * 128, :],
                                           waits=[yst_all[(sq, gt)],
                                                  (b.ve, ve_x2.get(tt - 2, 0))])
                            lnm2[tt] = ln_tile(ybs[:, v], xnst2[:, tt % 3],
                                               (lnscr2[:, v, 0:12], lnscr2[:, v, 12:14],
                                                lnscr2[:, v, 14:16]),
                                               [yt, (b.te, te_x2.get(tt - 3, 0))])
                        if tt >= 2:
                            t2 = tt - 2
                            tpm, vem = tm2fmF(xnst2[:, t2 % 3], xn2, t2,
                                              [(b.ve, lnm2[t2]), (b.te, b2_mm_end)])
                            te_x2[t2] = tpm
                            ve_x2[t2] = vem
                        yield 1024
                    xn2_done = vem
                    ve_m = {}
                    se_m = {}
                    for fs in range(16):
                        v, tok, it = witem((s_w1[fs], s_w2[fs]))
                        for sub in range(2):
                            fc = fs * 2 + sub
                            fa = f_next()
                            for ck in range(NCK):
                                mma = b.op("te", lambda E, fa=fa, v=v, sub=sub, ck=ck: E.matmul(
                                    F[fa],
                                    wstream[:, v, ck * 256 + sub * 128:ck * 256 + sub * 128 + 128],
                                    xn2[:, ck * 512:(ck + 1) * 512],
                                    start=(ck == 0), stop=(ck == NCK - 1)),
                                    waits=(ffree[fa] + [tok, (b.ve, xn2_done)])
                                    if ck == 0 else [],
                                    mark=(ck == NCK - 1))
                            fb = f_next()
                            for ck in range(NCK):
                                mmb = b.op("te", lambda E, fb=fb, v=v, sub=sub, ck=ck: E.matmul(
                                    F[fb],
                                    wstream[:, v, 2048 + ck * 256 + sub * 128:2048 + ck * 256 + sub * 128 + 128],
                                    xn2[:, ck * 512:(ck + 1) * 512],
                                    start=(ck == 0), stop=(ck == NCK - 1)),
                                    waits=(ffree[fb] + [tok]) if ck == 0 else [],
                                    mark=(ck == NCK - 1))
                            se_m[fc] = b.op("se", lambda E, fa=fa, fc=fc: E.activation(
                                stmp[:, fc % 2, :], F[fa], AF.Silu,
                                bias=zeroc[:], scale=s1c[:, fc:fc + 1]),
                                waits=[(b.te, mma), (b.ve, ve_m.get(fc - 2, 0))],
                                mark=True)
                            ffree[fa] = [(b.se, se_m[fc])]
                            ve_m[fc] = b.op("ve", lambda E, fb=fb, fc=fc: E.scalar_tensor_tensor(
                                u1[:, fc * 512:(fc + 1) * 512],
                                F[fb], s2c[:, fc:fc + 1],
                                stmp[:, fc % 2, :], A_mult, A_mult),
                                waits=[(b.te, mmb), (b.se, se_m[fc]),
                                       (b.te, quarter_te_end)], mark=True)
                            ffree[fb] = [(b.ve, ve_m[fc])]
                            yield 8192
                        wff_last[it] = mmb
                    b2_mm_end = mmb
                    b2_ve_end = b.ve.n

                    def hstage_g(h_scr, srcb, dstb, scvec, src_ready, dst_war):
                        semarks = []
                        for sl in range(32):
                            v, tok, it = witem((h_scr[sl],))
                            fa = f_next()
                            for ck in range(32):
                                mm = b.op("te", lambda E, fa=fa, v=v, ck=ck: E.matmul(
                                    F[fa],
                                    wstream[:, v, ck * 128:(ck + 1) * 128],
                                    srcb[:, ck * 512:(ck + 1) * 512],
                                    start=(ck == 0), stop=(ck == 31)),
                                    waits=(ffree[fa] + [tok] + src_ready) if ck == 0 else [],
                                    mark=(ck == 31))
                            sm = b.op("se", lambda E, fa=fa, sl=sl: E.activation(
                                dstb[:, sl * 512:(sl + 1) * 512], F[fa], AF.Gelu,
                                bias=zeroc[:], scale=scvec[:, sl:sl + 1]),
                                waits=[(b.te, mm)] + dst_war, mark=True)
                            ffree[fa] = [(b.se, sm)]
                            wff_last[it] = mm
                            semarks.append(sm)
                            yield 16384
                        return

                    yield from hstage_g(s_h1, u1, g1, h1c,
                                        [(b.ve, b2_ve_end)], [(b.te, quarter_te_end)])
                    b3_se_end = b.se.n
                    b3_mm_end = b.te.n
                    yield from hstage_g(s_h2, g1, u1, h2c,
                                        [(b.se, b3_se_end)], [(b.te, b3_mm_end)])
                    b4_se_end = b.se.n
                    b4_mm_end = b.te.n

                    w3_ve = {}
                    for sl in range(8):
                        v, tok, it = witem((s_w3[sl],))
                        fa = f_next()
                        for ck in range(32):
                            mm = b.op("te", lambda E, fa=fa, v=v, ck=ck: E.matmul(
                                F[fa],
                                wstream[:, v, ck * 128:(ck + 1) * 128],
                                u1[:, ck * 512:(ck + 1) * 512],
                                start=(ck == 0), stop=(ck == 31)),
                                waits=(ffree[fa] + [tok, (b.se, b4_se_end)])
                                if ck == 0 else [],
                                mark=(ck == 31))
                        w3_ve[sl] = b.op("ve", lambda E, fa=fa, sl=sl: E.tensor_scalar(
                            w3fm[:, sl * 512:(sl + 1) * 512], F[fa],
                            w3c[:, sl:sl + 1], None, A_mult, A_byp),
                            waits=[(b.te, mm), (b.te, b4_mm_end)], mark=True)
                        ffree[fa] = [(b.ve, w3_ve[sl])]
                        wff_last[it] = mm
                        yield 16384
                    w3_done = w3_ve[7]
                    ve_o = {}
                    for tt in range(4):
                        xrt = st_xr2.put(xos[:, 0],
                                         s_y[sq, th0 + tt * 128:th0 + (tt + 1) * 128, :],
                                         waits=[(b.ve, ve_o.get(tt - 1, 0)), out_tok[0]])
                        fa = f_next()
                        for c2 in range(NCK):
                            tpm = b.op("te", lambda E, fa=fa, c2=c2, tt=tt: E.transpose(
                                FEb[fa][:, c2 * 128:(c2 + 1) * 128],
                                w3fm[:, c2 * 512 + tt * 128:c2 * 512 + (tt + 1) * 128],
                                identb[:]),
                                waits=(ffree[fa] + [(b.ve, w3_done)]) if c2 == 0 else [],
                                mark=(c2 == NCK - 1))
                        ve_o[tt] = b.op("ve", lambda E, fa=fa: E.tensor_tensor(
                            xos[:, 1], FEb[fa], xos[:, 0], A_add),
                            waits=[(b.te, tpm), xrt, out_tok[1]], mark=True)
                        ffree[fa] = [(b.ve, ve_o[tt])]
                        tok = st_out.put(out[sq, th0 + tt * 128:th0 + (tt + 1) * 128, :],
                                         xos[:, 1], waits=[(b.ve, ve_o[tt])])
                        outst.append(tok)
                        out_tok[1] = tok
                        out_tok[0] = xrt
                        yield 1024
                    quarter_te_end = b.te.n
                yield -1

    WA = [490000, 820000, 820000, 820000][:n_seq]
    WB = [360000] + [5800000] * n_seq
    ga = gen_attn()
    gb = gen_ffn()
    a_done = [False]
    b_done = [False]

    def advance(g, done):
        try:
            return next(g)
        except StopIteration:
            done[0] = True
            return -1

    for rnd in range(n_seq + 1):
        a_alive = (rnd < n_seq) and not a_done[0]
        b_alive = not b_done[0]
        pa = 0.0
        pb = 0.0
        wa = WA[rnd] if rnd < n_seq else 1
        wb = WB[rnd]
        while a_alive or b_alive:
            if a_alive and (not b_alive or pa / wa <= pb / wb + 0.05):
                v = advance(ga, a_done)
                if v < 0:
                    a_alive = False
                else:
                    pa += v
            else:
                v = advance(gb, b_done)
                if v < 0:
                    b_alive = False
                else:
                    pb += v
    b.flush(blk)

    for (s, v) in outst:
        b.wait("gp", s, v)
    b.fence_all()
    b.flush(blk)
    blk.__exit__(None, None, None)
    nc.finalize()
    return nc


def make_consts():
    import ml_dtypes
    ident_f = np.eye(128, dtype=np.float32)
    ident_b = ident_f.astype(ml_dtypes.bfloat16)
    t = np.arange(128)
    cmask = np.where(t[:, None] <= t[None, :], 0.0, -1e9).astype(np.float32)
    return ident_b, ident_f, cmask


NAME_MAP = {
    "Wq": "L0_attn_Wq", "Wk": "L0_attn_Wk", "Wv": "L0_attn_Wv",
    "Aq": "L0_attn_Aq", "Ak": "L0_attn_Ak", "Av": "L0_attn_Av",
    "Bq": "L0_attn_Bq", "Bk": "L0_attn_Bk", "Bv": "L0_attn_Bv",
    "Wp": "L0_attn_Wp", "bp": "L0_attn_bp", "Ap": "L0_attn_Ap",
    "Bp": "L0_attn_Bp",
    "w1": "L0_ffn_w1_w", "s1": "L0_ffn_w1_s",
    "w2": "L0_ffn_w2_w", "s2": "L0_ffn_w2_s",
    "h1": "L0_ffn_h1_w", "hs1": "L0_ffn_h1_s",
    "h2": "L0_ffn_h2_w", "hs2": "L0_ffn_h2_s",
    "w3": "L0_ffn_w3_w", "s3": "L0_ffn_w3_s",
}


def kernel(**inputs):
    n_cores = 8
    x = np.ascontiguousarray(inputs["L0_attn_x"], dtype=np.float32)
    ident_b, ident_f, cmask = make_consts()
    base = {k: np.ascontiguousarray(inputs[v], dtype=np.float32)
            for k, v in NAME_MAP.items()}
    base["identb"] = ident_b
    base["identf"] = ident_f
    base["cmask"] = cmask
    n_seq = x.shape[0] // n_cores
    nc = build_core_program(n_seq=n_seq, T=x.shape[1])
    in_maps = []
    for i in range(n_cores):
        m = dict(base)
        m["x"] = np.ascontiguousarray(x[i * n_seq:(i + 1) * n_seq])
        in_maps.append(m)
    res = run_bass_kernel_spmd(nc, in_maps, core_ids=list(range(n_cores)))
    return np.concatenate([r["out"] for r in res.results], axis=0)


# version 21
# speedup vs baseline: 1.0419x; 1.0344x over previous
import numpy as np
from contextlib import ExitStack
import concourse.bass as bass
import concourse.mybir as mybir
from concourse.alu_op_type import AluOpType
from concourse.bass_utils import run_bass_kernel_spmd

F32 = mybir.dt.float32
BF16 = mybir.dt.bfloat16
AF = mybir.ActivationFunctionType
AX = mybir.AxisListType
A_add = AluOpType.add
A_sub = AluOpType.subtract
A_mult = AluOpType.mult
A_min = AluOpType.min
A_byp = AluOpType.bypass

ENGS = ("te", "ve", "se", "gp", "sy")
BLK = {"te": "tensor", "ve": "vector", "se": "scalar", "gp": "gpsimd", "sy": "sync"}

C = 1024
H = 16
HD = 64
R = 8
SCALING = 2.0
DH = 4096
NCK = C // 128
HP = H // 2
EPS = 1e-5


class Sem:
    __slots__ = ("h", "n")

    def __init__(self, nc, name):
        self.h = nc.alloc_semaphore(name)
        self.n = 0


class B:

    def __init__(self, nc):
        self.nc = nc
        self.prog = {e: [] for e in ENGS}
        self.lastw = {e: {} for e in ENGS}
        self.te = Sem(nc, "p_te")
        self.ve = Sem(nc, "p_ve")
        self.se = Sem(nc, "p_se")
        self.psem = {"te": self.te, "ve": self.ve, "se": self.se}
        self.dstreams = []

    def wait(self, eng, sem, val):
        if sem is None or val is None or val <= 0:
            return
        key = id(sem)
        if self.lastw[eng].get(key, 0) >= val:
            return
        self.lastw[eng][key] = val
        self.prog[eng].append(lambda E, h=sem.h, v=val: E.wait_ge(h, v))

    def op(self, eng, fn, waits=(), mark=False):
        for wv in waits:
            if wv is None:
                continue
            self.wait(eng, wv[0], wv[1])
        if mark:
            ps = self.psem[eng]
            ps.n += 1
            self.prog[eng].append(lambda E, fn=fn, h=ps.h: fn(E).then_inc(h, 1))
            return ps.n
        self.prog[eng].append(fn)
        return None

    def dma_raw(self, q, sem, dst, src, waits=()):
        sem.n += 16
        tgt = sem.n
        for wv in waits:
            if wv is None:
                continue
            self.wait(q, wv[0], wv[1])
        self.prog[q].append(
            lambda E, d=dst, s=src, h=sem.h: E.dma_start(d, s).then_inc(h, 16))
        return (sem, tgt)

    def flush(self, block):
        for e in ENGS:
            ops = self.prog[e]
            if not ops:
                continue
            self.prog[e] = []

            def body(E, ops=ops):
                for o in ops:
                    o(E)
            getattr(block, BLK[e])(body)

    def fence_tok(self):
        toks = [(self.te, self.te.n), (self.ve, self.ve.n), (self.se, self.se.n)]
        for ds in self.dstreams:
            for s in ds.sems:
                toks.append((s, s.n))
        return [t for t in toks if t[1] > 0]

    def fence_all(self):
        toks = self.fence_tok()
        for e in ENGS:
            for (s, v) in toks:
                self.wait(e, s, v)


class DS:

    def __init__(self, b, q, name, nbuf):
        self.b = b
        self.q = q
        self.sems = [Sem(b.nc, f"ds_{name}{i}") for i in range(nbuf)]
        self.cnt = 0
        b.dstreams.append(self)

    def put(self, dst, src, waits=(), new_item=True):
        if new_item:
            self.cnt += 1
        sem = self.sems[(self.cnt - 1) % len(self.sems)]
        return self.b.dma_raw(self.q, sem, dst, src, waits)


def build_core_program(n_seq=4, T=2048):
    assert T % 128 == 0 and T >= 128
    nc = bass.Bass("TRN2", target_bir_lowering=False, debug=False)
    NT = T // 128
    TCW = min(512, T)
    NTC = T // TCW
    THALF = min(1024, T)
    NHALF = T // THALF
    NTH = THALF // 128
    TCW2 = min(512, THALF)
    NTC2 = THALF // TCW2

    def din(name, shape):
        return nc.dram_tensor(name, shape, F32, kind="ExternalInput").ap()

    x_in = din("x", [n_seq, T, C])
    w_in = {k: din(k, [H, C, HD]) for k in ("Wq", "Wk", "Wv")}
    a_in = {k: din(k, [H, C, R]) for k in ("Aq", "Ak", "Av")}
    bb_in = {k: din(k, [H, R, HD]) for k in ("Bq", "Bk", "Bv")}
    wp_in = din("Wp", [C, C])
    bp_in = din("bp", [C])
    ap_in = din("Ap", [C, R])
    bpp_in = din("Bp", [R, C])
    w1_in = din("w1", [DH, C])
    w2_in = din("w2", [DH, C])
    h1_in = din("h1", [DH, DH])
    h2_in = din("h2", [DH, DH])
    w3_in = din("w3", [C, DH])
    s1_in = din("s1", [DH])
    s2_in = din("s2", [DH])
    hs1_in = din("hs1", [DH])
    hs2_in = din("hs2", [DH])
    s3_in = din("s3", [C])
    identb_in = nc.dram_tensor("identb", [128, 128], BF16, kind="ExternalInput").ap()
    identf_in = nc.dram_tensor("identf", [128, 128], F32, kind="ExternalInput").ap()
    cmask_in = nc.dram_tensor("cmask", [128, 128], F32, kind="ExternalInput").ap()

    out = nc.dram_tensor("out", [n_seq, T, C], F32, kind="ExternalOutput").ap()

    s_wqkv = {k: nc.dram_tensor(f"s_{k}", [HP, 128, NCK * 128], BF16, kind="Internal").ap()
              for k in ("Wq", "Wk", "Wv")}
    s_wp = nc.dram_tensor("s_wp", [NCK, 128, NCK * 128], BF16, kind="Internal").ap()
    s_w1 = nc.dram_tensor("s_w1", [16, 128, NCK * 256], BF16, kind="Internal").ap()
    s_w2 = nc.dram_tensor("s_w2", [16, 128, NCK * 256], BF16, kind="Internal").ap()
    s_h1 = nc.dram_tensor("s_h1", [16, 128, 32 * 256], BF16, kind="Internal").ap()
    s_h2 = nc.dram_tensor("s_h2", [16, 128, 32 * 256], BF16, kind="Internal").ap()
    s_w3 = nc.dram_tensor("s_w3", [4, 128, 32 * 256], BF16, kind="Internal").ap()
    s_y = nc.dram_tensor("s_y", [n_seq, T, C], F32, kind="Internal").ap()

    b = B(nc)
    _uid = [0]

    def sbt(name, shape, dtype):
        _uid[0] += 1
        return nc.sbuf_tensor(f"{name}_{_uid[0]}", shape, dtype)

    st_const = DS(b, "sy", "const", 1)
    st_weff = DS(b, "gp", "weff", 2)
    st_wst = DS(b, "sy", "wst", 2)
    st_wpn = DS(b, "gp", "wpn", 2)
    st_apbp = DS(b, "gp", "apbp", 1)
    st_fnat = DS(b, "gp", "fnat", 2)
    st_fst = DS(b, "sy", "fst", 2)
    st_x = DS(b, "sy", "x", 2)
    st_wqkv = DS(b, "sy", "wqkv", 2)
    st_wpj = DS(b, "sy", "wpj", 2)
    st_xr = DS(b, "sy", "xr", 2)
    st_ys = DS(b, "sy", "ys", 2)
    st_yb = DS(b, "sy", "yb", 2)
    st_w12 = DS(b, "sy", "w12", 2)
    st_hsl = DS(b, "sy", "hsl", 2)
    st_w3 = DS(b, "sy", "w3", 1)
    st_xr2 = DS(b, "sy", "xr2", 2)
    st_out = DS(b, "gp", "outq", 2)

    identb = nc.alloc_sbuf_tensor("identb_sb", [128, 128], BF16)
    identf = nc.alloc_sbuf_tensor("identf_sb", [128, 128], F32)
    cmask = nc.alloc_sbuf_tensor("cmask_sb", [128, 128], F32)
    s1c = nc.alloc_sbuf_tensor("s1c", [128, 32], F32)
    s2c = nc.alloc_sbuf_tensor("s2c", [128, 32], F32)
    h1c = nc.alloc_sbuf_tensor("h1c", [128, 32], F32)
    h2c = nc.alloc_sbuf_tensor("h2c", [128, 32], F32)
    w3c = nc.alloc_sbuf_tensor("w3c", [128, 8], F32)
    bpc = nc.alloc_sbuf_tensor("bpc", [128, 8], F32)
    epsc = nc.alloc_sbuf_tensor("epsc", [128, 1], F32)
    zeroc = nc.alloc_sbuf_tensor("zeroc", [128, 1], F32)
    allones = nc.alloc_sbuf_tensor("allones", [128, 1], BF16)
    onesb2 = nc.alloc_sbuf_tensor("onesb2", [128, 2], BF16)
    onesf = nc.alloc_sbuf_tensor("onesf", [128, 128], F32)
    negonesf = nc.alloc_sbuf_tensor("negonesf", [128, 128], F32)

    PSE = nc.alloc_psum_tensor("PSE", [128, 2048], F32)
    PSO = nc.alloc_psum_tensor("PSO", [128, 2048], F32)
    PSEb = PSE.bitcast(BF16)
    PSOb = PSO.bitcast(BF16)
    PS = (PSE, PSO)
    PSb = (PSEb, PSOb)

    blk = nc.Block()
    blk.__enter__()

    st_const.put(identb[:], identb_in)
    st_const.put(identf[:], identf_in)
    st_const.put(cmask[:], cmask_in)
    b.op("ve", lambda E: E.memset(epsc[:], EPS), mark=True)
    b.op("ve", lambda E: E.memset(zeroc[:], 0.0), mark=True)
    b.op("ve", lambda E: E.memset(allones[:], 1.0), mark=True)
    b.op("ve", lambda E: E.memset(onesb2[:], 0.0), mark=True)
    b.op("ve", lambda E: E.memset(onesb2[0:64, 0:1], 1.0), mark=True)
    b.op("ve", lambda E: E.memset(onesb2[64:128, 1:2], 1.0), mark=True)
    b.op("ve", lambda E: E.memset(onesf[:], 1.0), mark=True)
    b.op("ve", lambda E: E.memset(negonesf[:], -1.0), mark=True)

    with sbt("vstage", [32, 6 * 128], F32) as vstage:
        vspecs = [(s1_in, s1c, 32), (s2_in, s2c, 32), (hs1_in, h1c, 32),
                  (hs2_in, h2c, 32), (s3_in, w3c, 8), (bp_in, bpc, 8)]
        for i, (src, dst, nn) in enumerate(vspecs):
            st_const.put(vstage[0:nn, i * 128:(i + 1) * 128],
                         src.rearrange("(a p) -> a p", p=128))
        t_const = (st_const.sems[0], st_const.sems[0].n)
        t_idb = t_idf = t_msk = t_const
        vcp = 0
        for i, (src, dst, nn) in enumerate(vspecs):
            tp = b.op("te", lambda E, i=i, nn=nn: E.transpose(
                PSE[:, i * 32:i * 32 + nn],
                vstage[0:nn, i * 128:(i + 1) * 128], identf[0:nn, 0:nn]),
                waits=[t_const, (b.ve, vcp)], mark=True)
            vcp = b.op("ve", lambda E, dst=dst, i=i, nn=nn: E.tensor_copy(
                dst[:, 0:nn], PSE[:, i * 32:i * 32 + nn]),
                waits=[(b.te, tp)], mark=True)
        b.flush(blk)

    b.fence_all()
    with sbt("wnat", [128, 2, NCK * 128], BF16) as wnat, \
         sbt("anat", [128, 2, 2 * NCK * R], BF16) as anat, \
         sbt("bnat", [R, 2, 128], BF16) as bnat, \
         sbt("aT", [R, 2, 2 * NCK * 128], BF16) as aT, \
         sbt("wst", [128, 2, NCK * 128], BF16) as wst:
        te_mm = {}
        ve_stt = {}
        te_tp = {}
        ve_aT = {}
        wst_tok = {}
        for kk, (key, ka, kb2) in enumerate((("Wq", "Aq", "Bq"), ("Wk", "Ak", "Bk"),
                                             ("Wv", "Av", "Bv"))):
            wsrc, asrc, bsrc, dst = w_in[key], a_in[ka], bb_in[kb2], s_wqkv[key]
            for hp in range(HP):
                g = kk * HP + hp
                v = g % 2
                itemw = [(b.ve, ve_stt.get(g - 2, 0)), (b.te, te_tp.get(g - 2, 0)),
                         (b.te, te_mm.get(g - 2, 0))]
                tg = None
                for par in range(2):
                    h = 2 * hp + par
                    st_weff.put(
                        wnat[:, v].rearrange("p (ck q d) -> p ck q d", ck=NCK, q=2)[:, :, par],
                        wsrc[h].rearrange("(ck p) d -> p ck d", p=128),
                        waits=itemw, new_item=(par == 0))
                    st_weff.put(
                        anat[:, v].rearrange("p (q ck r) -> p q ck r", q=2, ck=NCK)[:, par],
                        asrc[h].rearrange("(ck p) r -> p ck r", p=128), new_item=False)
                    tg = st_weff.put(
                        bnat[:, v].rearrange("r (q d) -> r q d", q=2)[:, par],
                        bsrc[h], new_item=False)
                tw = [tg, tg]
                ta = [tg, tg]
                tb = [tg, tg]
                for par in range(2):
                    for ck in range(NCK):
                        j = par * NCK + ck
                        tpm = b.op("te", lambda E, v=v, j=j: E.transpose(
                            PSOb[0:R, v * 2048 + j * 128:v * 2048 + (j + 1) * 128],
                            anat[:, v, j * R:(j + 1) * R], identb[:]),
                            waits=[ta[par], t_const, (b.ve, ve_aT.get(g - 2, 0))],
                            mark=(j == 2 * NCK - 1))
                te_tp[g] = tpm
                ve_aT[g] = b.op("ve", lambda E, v=v: E.tensor_copy(
                    aT[0:R, v], PSOb[0:R, v * 2048:(v + 1) * 2048]),
                    waits=[(b.te, tpm)], mark=True)
                for par in range(2):
                    for ck in range(NCK):
                        j = par * NCK + ck
                        col = v * 1024 + ck * 128 + par * 64
                        mm = b.op("te", lambda E, j=j, col=col, v=v, par=par: E.matmul(
                            PSE[:, col:col + 64],
                            aT[0:R, v, j * 128:(j + 1) * 128],
                            bnat[0:R, v, par * 64:(par + 1) * 64],
                            start=True, stop=True),
                            waits=[(b.ve, ve_aT[g]), tb[par],
                                   (b.ve, ve_stt.get(g - 2, 0))],
                            mark=(j == 2 * NCK - 1))
                te_mm[g] = mm
                ve_stt[g] = b.op("ve", lambda E, v=v: E.scalar_tensor_tensor(
                    wst[:, v], PSE[:, v * 1024:(v + 1) * 1024], SCALING,
                    wnat[:, v], A_mult, A_add),
                    waits=[(b.te, mm), wst_tok.get(g - 2, None)] + tw, mark=True)
                wst_tok[g] = st_wst.put(dst[hp], wst[:, v], waits=[(b.ve, ve_stt[g])])

        glast = 3 * HP - 1
        for ck in range(NCK):
            st_apbp.put(anat[:, 0, ck * R:(ck + 1) * R],
                        ap_in.rearrange("(ck p) r -> p ck r", p=128)[:, ck],
                        waits=[(b.te, te_tp.get(glast, 0)), (b.te, te_mm.get(glast, 0)),
                               (b.ve, ve_stt.get(glast, 0))], new_item=(ck == 0))
        bp_tok = st_apbp.put(aT[0:R, 0, 0:C], bpp_in, new_item=False)
        ap_toks = [bp_tok] * NCK
        for ck in range(NCK):
            tpm = b.op("te", lambda E, ck=ck: E.transpose(
                PSOb[0:R, ck * 128:(ck + 1) * 128],
                anat[:, 0, ck * R:(ck + 1) * R], identb[:]),
                waits=[ap_toks[ck], (b.ve, ve_aT.get(glast, 0))],
                mark=(ck == NCK - 1))
        apT_ev = b.op("ve", lambda E: E.tensor_copy(
            aT[0:R, 1, 0:NCK * 128], PSOb[0:R, 0:NCK * 128]),
            waits=[(b.te, tpm), (b.te, te_mm.get(glast, 0))], mark=True)
        wp_mm = {}
        wpst_tok = {-1: wst_tok.get(glast, None), -2: wst_tok.get(glast - 1, None)}
        wp_stt = {glast: ve_stt.get(glast, 0)}
        for c2 in range(NCK):
            v = c2 % 2
            tw = st_wpn.put(wnat[:, v].rearrange("p (ck f) -> p ck f", ck=NCK),
                            wp_in.rearrange("(ck p) (c2 f) -> p ck c2 f", p=128, f=128)[:, :, c2],
                            waits=[(b.ve, wp_stt.get(c2 - 2, ve_stt.get(glast, 0)))])
            for ck in range(NCK):
                mm = b.op("te", lambda E, v=v, ck=ck, c2=c2: E.matmul(
                    PSE[:, v * 1024 + ck * 128:v * 1024 + (ck + 1) * 128],
                    aT[0:R, 1, ck * 128:(ck + 1) * 128],
                    aT[0:R, 0, c2 * 128:(c2 + 1) * 128],
                    start=True, stop=True),
                    waits=[(b.ve, apT_ev), bp_tok,
                           (b.ve, wp_stt.get(c2 - 2, 0))],
                    mark=(ck == NCK - 1))
            wp_mm[c2] = mm
            wp_stt[c2] = b.op("ve", lambda E, v=v: E.scalar_tensor_tensor(
                wst[:, v], PSE[:, v * 1024:(v + 1) * 1024], SCALING,
                wnat[:, v], A_mult, A_add),
                waits=[(b.te, mm), tw, wpst_tok.get(c2 - 2, wst_tok.get(glast - 1, None))],
                mark=True)
            wpst_tok[c2] = st_wst.put(s_wp[c2], wst[:, v], waits=[(b.ve, wp_stt[c2])])
        b.flush(blk)

    setup_toks2 = b.fence_tok()

    def emit_ffnt(fnat, fstage):
        FTb = [PSEb[:, 0:1024], PSEb[:, 1024:2048]]
        specs = [(w1_in, s_w1, C, 16), (w2_in, s_w2, C, 16),
                 (h1_in, s_h1, DH, 16), (h2_in, s_h2, DH, 16), (w3_in, s_w3, DH, 4)]
        gsl = 0
        te_last = {}
        prev_ev = {}
        st_t = {}
        for (wsrc, wdst, I, NSL) in specs:
            NG = I // 512
            for sl in range(NSL):
                v = gsl % 2
                tnat = st_fnat.put(fnat[:, v, 0:2 * I].rearrange(
                    "p (oc i) -> p oc i", oc=2),
                    wsrc[sl * 256:(sl + 1) * 256, :].rearrange(
                        "(oc p) i -> p oc i", p=128),
                    waits=[(b.te, te_last.get(gsl - 2, 0))])
                for g in range(NG):
                    pv = g % 2
                    for k in range(8):
                        ic = g * 4 + k // 2
                        oc = k % 2
                        tpm = b.op("te", lambda E, pv=pv, k=k, ic=ic, oc=oc, v=v, I=I: E.transpose(
                            FTb[pv][:, k * 128:(k + 1) * 128],
                            fnat[:, v, oc * I + ic * 128:oc * I + (ic + 1) * 128],
                            identb[:]),
                            waits=(setup_toks2 + [tnat, t_const, (b.ve, prev_ev.get(pv, 0))]) if k == 0 else [],
                            mark=(k == 7))
                    prev_ev[pv] = b.op("ve", lambda E, pv=pv, v=v, g=g: E.tensor_copy(
                        fstage[:, v, g * 1024:(g + 1) * 1024], FTb[pv][:, 0:1024]),
                        waits=[(b.te, tpm), st_t.get(gsl - 2, None)], mark=True)
                te_last[gsl] = tpm
                st_t[gsl] = st_fst.put(wdst[sl], fstage[:, v, 0:NG * 1024],
                                       waits=[(b.ve, prev_ev[(NG - 1) % 2])])
                gsl += 1
        b.flush(blk)

    SC = [PSO[:, 0:512], PSO[:, 512:1024], PSO[:, 1024:1536]]
    SCb = [PSOb[:, 0:1024], PSOb[:, 1024:2048], PSOb[:, 2048:3072]]
    Cb_ = PSO[:, 1536:2048]
    Db_ = PSE[:, 1536:2048]
    scfree = {0: [], 1: [], 2: []}
    scn = [0]

    def sc_reset():
        for i in range(3):
            scfree[i] = []

    def sc_next():
        i = scn[0] % 3
        scn[0] += 1
        return i

    def tm2fm2(src_bf, dst_fm, tau, waits_src):
        i = sc_next()
        for ck in range(NCK):
            tpm = b.op("te", lambda E, ck=ck, i=i: E.transpose(
                SCb[i][:, ck * 128:(ck + 1) * 128],
                src_bf[:, ck * 128:(ck + 1) * 128], identb[:]),
                waits=(list(waits_src) + [t_const] + scfree[i]) if ck == 0 else [],
                mark=(ck == NCK - 1))
        vem = b.op("ve", lambda E, i=i, tau=tau: E.tensor_copy(
            dst_fm.rearrange("p (ck t) -> p ck t", ck=NCK)[:, :, tau * 128:(tau + 1) * 128],
            SCb[i].rearrange("p (ck j) -> p ck j", ck=NCK)),
            waits=[(b.te, tpm)], mark=True)
        scfree[i] = [(b.ve, vem)]
        return tpm, vem

    def ln_tile(x_view, dst_view, scr, waits):
        bn, mv, st = scr
        b.op("ve", lambda E: E.bn_stats(bn[:, 0:6], x_view[:, 0:512]),
             waits=waits, mark=True)
        m2 = b.op("ve", lambda E: E.bn_stats(bn[:, 6:12], x_view[:, 512:1024]),
                  mark=True)
        m3 = b.op("ve", lambda E: E.bn_aggr(mv, bn), waits=[(b.ve, m2)], mark=True)
        m4 = b.op("se", lambda E: E.activation(st[:, 0:1], mv[:, 1:2], AF.Sqrt,
                                               bias=epsc[:], scale=1.0),
                  waits=[(b.ve, m3)], mark=True)
        m5 = b.op("ve", lambda E: E.reciprocal(st[:, 1:2], st[:, 0:1]),
                  waits=[(b.se, m4)], mark=True)
        m6 = b.op("ve", lambda E: E.tensor_scalar(
            dst_view, x_view, mv[:, 0:1], st[:, 1:2], A_sub, A_mult),
            waits=[(b.ve, m5)], mark=True)
        return m6

    outst = []
    for sq in range(n_seq):
        b.fence_all()
        sc_reset()
        with sbt("xn_fm", [128, NCK * T], BF16) as xn_fm, \
             sbt("at_fm", [128, NCK * T], BF16) as at_fm:
            _fs = ExitStack()
            if sq == 0:
                fnat0 = _fs.enter_context(sbt("fnat", [128, 2, 2 * DH], BF16))
                fstage0 = _fs.enter_context(sbt("fstage", [128, 2, 32 * 256], BF16))
            with sbt("xa", [128, 2, 1024], F32) as xa, \
                 sbt("xnst", [128, 2, 1024], BF16) as xnst, \
                 sbt("lnscr", [128, 2, 16], F32) as lnscr:
                ve_ts = {}
                te_tp2 = {}
                a1_ev = {}
                for tau in range(NT):
                    vv = tau % 2
                    xat = st_x.put(xa[:, vv],
                                   x_in[sq, tau * 128:(tau + 1) * 128, :],
                                   waits=[(b.ve, ve_ts.get(tau - 2, 0))])
                    m6 = ln_tile(xa[:, vv], xnst[:, vv],
                                 (lnscr[:, vv, 0:12], lnscr[:, vv, 12:14],
                                  lnscr[:, vv, 14:16]),
                                 [xat, (b.te, te_tp2.get(tau - 2, 0))])
                    ve_ts[tau] = m6
                    tpm, vem = tm2fm2(xnst[:, vv], xn_fm, tau, [(b.ve, m6)])
                    te_tp2[tau] = tpm
                    a1_ev[tau] = vem
                b.flush(blk)

            if sq == 0:
                emit_ffnt(fnat0, fstage0)
                b.flush(blk)
            _fs.close()

            b.fence_all()
            sc_reset()
            with sbt("wqkv", [128, 2, 3 * NCK * 128], BF16) as wqkv, \
                 sbt("qb", [128, 2, T], BF16) as qb, \
                 sbt("kb", [128, 2, T], BF16) as kb, \
                 sbt("vf", [128, 2, T], BF16) as vf, \
                 sbt("vt", [128, 2, T], BF16) as vt, \
                 sbt("psb", [128, 4, 512], BF16) as psb, \
                 sbt("sqb", [128, 2 * T], BF16) as sqb, \
                 sbt("mxs", [128, 2, 16], F32) as mxs, \
                 sbt("zr", [128, 2, 512], F32) as zr, \
                 sbt("bcs", [128, 2, 512], F32) as bcs, \
                 sbt("bB", [128, 2, 2], F32) as bB:
                te_hp_end = {}
                qkv_mm_end = {}
                bounds_red = {}
                psb_last = {}
                zr_last = {}
                bcs_last = {}
                last_ve01 = {0: 0, 1: 0}
                for hp in range(HP):
                    vb = hp % 2
                    wt = [st_wqkv.put(wqkv[:, vb, wi * NCK * 128:(wi + 1) * NCK * 128],
                                      s_wqkv[key][hp],
                                      waits=[(b.te, qkv_mm_end.get(hp - 2, 0))],
                                      new_item=(wi == 0))
                          for wi, key in enumerate(("Wq", "Wk", "Wv"))]
                    wt = wt[2]
                    for wi in range(3):
                        dstb, qsc = ((qb, 0.125), (kb, 1.0), (vf, 1.0))[wi]
                        for tc in range(NTC):
                            i = sc_next()
                            for ck in range(NCK):
                                mm = b.op("te", lambda E, i=i, vb=vb, wi=wi, ck=ck, tc=tc: E.matmul(
                                    SC[i],
                                    wqkv[:, vb, wi * NCK * 128 + ck * 128:wi * NCK * 128 + (ck + 1) * 128],
                                    xn_fm[:, ck * T + tc * 512:ck * T + (tc + 1) * 512],
                                    start=(ck == 0), stop=(ck == NCK - 1)),
                                    waits=(scfree[i] + [wt, (b.ve, a1_ev[min(tc * 4 + 3, NT - 1)])])
                                    if ck == 0 else [],
                                    mark=(ck == NCK - 1))
                            sm = b.op("se", lambda E, i=i, vb=vb, tc=tc, dstb=dstb, qsc=qsc: E.activation(
                                dstb[:, vb, tc * 512:(tc + 1) * 512], SC[i],
                                AF.Copy, bias=0.0, scale=qsc),
                                waits=[(b.te, mm), (b.te, te_hp_end.get(hp - 2, 0))],
                                mark=True)
                            scfree[i] = [(b.se, sm)]
                    qkv_mm_end[hp] = mm
                    qkv_evac = b.se.n
                    for bt in range(2):
                        i = sc_next()
                        for k in range(8):
                            st = bt * 8 + k
                            tpm = b.op("te", lambda E, i=i, k=k, st=st, vb=vb: E.transpose(
                                SCb[i][:, k * 128:(k + 1) * 128],
                                vf[:, vb, st * 128:(st + 1) * 128], identb[:]),
                                waits=(scfree[i] + [(b.se, qkv_evac)]) if k == 0 else [],
                                mark=(k == 7))
                        vem = b.op("ve", lambda E, i=i, bt=bt, vb=vb: E.tensor_copy(
                            vt[:, vb, bt * 1024:(bt + 1) * 1024], SCb[i]),
                            waits=[(b.te, tpm), (b.te, te_hp_end.get(hp - 2, 0))],
                            mark=True)
                        scfree[i] = [(b.ve, vem)]
                    vt_ready = vem
                    i = sc_next()
                    sqm1 = b.op("se", lambda E, vb=vb: E.activation(
                        sqb[:, 0:T], qb[:, vb], AF.Square, bias=zeroc[:], scale=1.0),
                        waits=[(b.ve, bounds_red.get(hp - 1, 0))], mark=True)
                    sqm2 = b.op("se", lambda E, vb=vb: E.activation(
                        sqb[:, T:2 * T], kb[:, vb], AF.Square, bias=zeroc[:], scale=1.0),
                        mark=True)
                    rm = 0
                    for tc8 in range(8):
                        nm = b.op("te", lambda E, i=i, tc8=tc8: E.matmul(
                            SC[i][0:2, 0:512], onesb2[:, 0:2],
                            sqb[:, tc8 * 512:(tc8 + 1) * 512],
                            start=True, stop=True),
                            waits=(scfree[i] + [(b.se, sqm1 if tc8 < 4 else sqm2)])
                            if tc8 == 0 else [(b.se, sqm2), (b.ve, rm)],
                            mark=True)
                        rm = b.op("ve", lambda E, i=i, vb=vb, tc8=tc8: E.reduce_max(
                            mxs[0:2, vb, tc8:tc8 + 1], SC[i][0:2, 0:512], axis=AX.X),
                            waits=[(b.te, nm)], mark=True)
                    r1 = b.op("ve", lambda E, vb=vb: E.reduce_max(
                        mxs[0:2, vb, 8:9], mxs[0:2, vb, 0:4], axis=AX.X),
                        waits=[(b.ve, rm)], mark=True)
                    r2 = b.op("ve", lambda E, vb=vb: E.reduce_max(
                        mxs[0:2, vb, 9:10], mxs[0:2, vb, 4:8], axis=AX.X),
                        mark=True)
                    bounds_red[hp] = r2
                    r3 = b.op("ve", lambda E, vb=vb: E.tensor_tensor(
                        mxs[0:2, vb, 10:11], mxs[0:2, vb, 8:9], mxs[0:2, vb, 9:10],
                        A_mult),
                        waits=[(b.ve, r2)], mark=True)
                    tp2 = b.op("te", lambda E, i=i, vb=vb: E.transpose(
                        SC[i][0:1, 128:130], mxs[0:2, vb, 10:11], identf[0:2, 0:2]),
                        waits=[(b.ve, r3)], mark=True)
                    r4 = b.op("ve", lambda E, i=i, vb=vb: E.tensor_copy(
                        mxs[0:1, vb, 11:13], SC[i][0:1, 128:130]),
                        waits=[(b.te, tp2)], mark=True)
                    r5 = b.op("se", lambda E, vb=vb: E.activation(
                        mxs[0:1, vb, 11:13], mxs[0:1, vb, 11:13], AF.Sqrt,
                        bias=zeroc[0:1, 0:1], scale=1.0),
                        waits=[(b.ve, r4)], mark=True)
                    bcm = {}
                    for par in range(2):
                        bcm[par] = b.op("te", lambda E, i=i, vb=vb, par=par: E.matmul(
                            SC[i][:, 256 + par:257 + par], negonesf[0:1, 0:128],
                            mxs[0:1, vb, 11 + par:12 + par],
                            start=True, stop=True),
                            waits=[(b.se, r5)], mark=(par == 1))
                    bBm = b.op("se", lambda E, i=i, vb=vb: E.activation(
                        bB[:, vb, 0:2], SC[i][:, 256:258], AF.Copy, bias=0.0, scale=1.0),
                        waits=[(b.te, bcm[1])], mark=True)
                    scfree[i] = [(b.se, bBm)]

                    for tau2 in range(4):
                        ns = 4 * tau2 + 4
                        exp_m = {}
                        av_m = {}
                        for step in range(ns + 2):
                            if step >= 2:
                                sg = step - 2
                                dlt = sg - 4 * tau2
                                off = 128 * dlt if dlt >= 0 else 0
                                for par in range(2):
                                    p0 = par * 64
                                    slot = (sg % 2) * 2 + par
                                    Xb = Cb_ if par == 0 else Db_
                                    pv = 0 if par == 0 else 64
                                    zrow = 64 if par == 0 else 0
                                    avw = [(b.se, exp_m[(sg, par)]), (b.ve, vt_ready)]
                                    if sg == 0:
                                        avw.append((b.ve, last_ve01[par]))
                                    avm = b.op("te", lambda E, Xb=Xb, pv=pv, vb=vb, sg=sg, p0=p0, slot=slot, off=off, ns=ns: E.matmul(
                                        Xb[pv:pv + 64, off:512],
                                        vt[:, vb, sg * 128 + p0:sg * 128 + p0 + 64],
                                        psb[:, slot, off:512],
                                        start=(sg == 0), stop=(sg == ns - 1),
                                        skip_group_check=True),
                                        waits=avw, mark=True)
                                    zm = b.op("te", lambda E, Xb=Xb, zrow=zrow, slot=slot, off=off, sg=sg, ns=ns: E.matmul(
                                        Xb[zrow:zrow + 1, off:512],
                                        allones[:, 0:1],
                                        psb[:, slot, off:512],
                                        start=(sg == 0), stop=(sg == ns - 1),
                                        skip_group_check=True),
                                        mark=True)
                                    av_m[(sg, par)] = zm
                                    psb_last[slot] = zm
                            if step < ns:
                                sg = step
                                dlt = sg - 4 * tau2
                                off = 128 * dlt if dlt >= 0 else 0
                                for par in range(2):
                                    p0 = par * 64
                                    i = sc_next()
                                    spm = b.op("te", lambda E, i=i, vb=vb, p0=p0, sg=sg, tau2=tau2, off=off: E.matmul(
                                        SC[i][:, off:512],
                                        kb[p0:p0 + 64, vb, sg * 128:(sg + 1) * 128],
                                        qb[p0:p0 + 64, vb, tau2 * 512 + off:(tau2 + 1) * 512],
                                        start=True, stop=True),
                                        waits=scfree[i] + [(b.se, qkv_evac)],
                                        mark=True)
                                    if dlt >= 0:
                                        mkm = b.op("ve", lambda E, i=i, off=off: E.tensor_tensor(
                                            SC[i][:, off:off + 128], SC[i][:, off:off + 128],
                                            cmask[:], A_add),
                                            waits=[(b.te, spm), t_msk], mark=True)
                                        ew = [(b.ve, mkm)]
                                    else:
                                        ew = [(b.te, spm)]
                                    slot = (sg % 2) * 2 + par
                                    em = b.op("se", lambda E, i=i, vb=vb, par=par, slot=slot, off=off: E.activation(
                                        psb[:, slot, off:512], SC[i][:, off:512],
                                        AF.Exp, bias=bB[:, vb, par:par + 1], scale=1.0),
                                        waits=ew + [(b.se, bBm),
                                                    (b.te, psb_last.get(slot, 0))],
                                        mark=True)
                                    exp_m[(sg, par)] = em
                                    scfree[i] = [(b.se, em)]
                        zsl = tau2 % 2
                        rc0 = b.op("ve", lambda E, zsl=zsl: E.reciprocal(
                            zr[64:65, zsl, :], Cb_[64:65, 0:512]),
                            waits=[(b.te, av_m[(ns - 1, 0)]),
                                   (b.te, zr_last.get(zsl, 0))], mark=True)
                        rc1 = b.op("ve", lambda E, zsl=zsl: E.reciprocal(
                            zr[0:1, zsl, :], Db_[0:1, 0:512]),
                            waits=[(b.te, av_m[(ns - 1, 1)])], mark=True)
                        iX = sc_next()
                        iY = sc_next()
                        bc0 = b.op("te", lambda E, iX=iX, zsl=zsl: E.matmul(
                            SC[iX][0:64, 0:512], onesf[64:65, 0:64], zr[64:65, zsl, :],
                            start=True, stop=True),
                            waits=scfree[iX] + [(b.ve, rc0)], mark=True)
                        bc1 = b.op("te", lambda E, iY=iY, zsl=zsl: E.matmul(
                            SC[iY][64:128, 0:512], onesf[0:1, 0:64], zr[0:1, zsl, :],
                            start=True, stop=True),
                            waits=scfree[iY] + [(b.ve, rc1)], mark=True)
                        zr_last[zsl] = bc1
                        sb0 = b.op("se", lambda E, iX=iX, zsl=zsl: E.activation(
                            bcs[0:64, zsl, :], SC[iX][0:64, 0:512],
                            AF.Copy, bias=0.0, scale=1.0),
                            waits=[(b.te, bc0), (b.ve, bcs_last.get(zsl, 0))],
                            mark=True)
                        sb1 = b.op("se", lambda E, iY=iY, zsl=zsl: E.activation(
                            bcs[64:128, zsl, :], SC[iY][64:128, 0:512],
                            AF.Copy, bias=0.0, scale=1.0),
                            waits=[(b.te, bc1)], mark=True)
                        ve0 = b.op("ve", lambda E, hp=hp, tau2=tau2, zsl=zsl: E.tensor_tensor(
                            at_fm[0:64, hp * T + tau2 * 512:hp * T + (tau2 + 1) * 512],
                            Cb_[0:64, 0:512], bcs[0:64, zsl, :], A_mult),
                            waits=[(b.se, sb0)], mark=True)
                        ve1 = b.op("ve", lambda E, hp=hp, tau2=tau2, zsl=zsl: E.tensor_tensor(
                            at_fm[64:128, hp * T + tau2 * 512:hp * T + (tau2 + 1) * 512],
                            Db_[64:128, 0:512], bcs[64:128, zsl, :], A_mult),
                            waits=[(b.se, sb1)], mark=True)
                        last_ve01[0] = ve0
                        last_ve01[1] = ve1
                        bcs_last[zsl] = ve1
                        scfree[iX] = [(b.se, sb0)]
                        scfree[iY] = [(b.se, sb1)]
                    te_hp_end[hp] = b.te.n
                b.flush(blk)

            b.fence_all()
            sc_reset()
            pfm = xn_fm
            with sbt("wpj", [128, 2, NCK * 128], BF16) as wpj, \
                 sbt("xr", [128, 2, 1024], F32) as xr, \
                 sbt("ys", [128, 2, 1024], F32) as ys:
                wp_t = {}
                te_pmm = {}
                for c2 in range(NCK):
                    side = c2 % 2
                    wp_t[c2] = st_wpj.put(wpj[:, side], s_wp[c2],
                                          waits=[(b.te, te_pmm.get(c2 - 2, 0))])
                    for tc in range(NTC):
                        i = sc_next()
                        for ck in range(NCK):
                            mm = b.op("te", lambda E, i=i, side=side, ck=ck, tc=tc: E.matmul(
                                SC[i],
                                wpj[:, side, ck * 128:(ck + 1) * 128],
                                at_fm[:, ck * T + tc * 512:ck * T + (tc + 1) * 512],
                                start=(ck == 0), stop=(ck == NCK - 1)),
                                waits=(scfree[i] + [wp_t[c2]]) if ck == 0 else [],
                                mark=(ck == NCK - 1))
                        pv = b.op("ve", lambda E, i=i, c2=c2, tc=tc: E.tensor_scalar(
                            pfm[:, c2 * T + tc * 512:c2 * T + (tc + 1) * 512],
                            SC[i], bpc[:, c2:c2 + 1], None, A_add, A_byp),
                            waits=[(b.te, mm)], mark=True)
                        scfree[i] = [(b.ve, pv)]
                    te_pmm[c2] = mm
                pev_done = b.ve.n
                yst = {}
                ve_yt = {}
                for tau in range(NT):
                    v = tau % 2
                    xrt = st_xr.put(xr[:, v], x_in[sq, tau * 128:(tau + 1) * 128, :],
                                    waits=[(b.ve, ve_yt.get(tau - 2, 0))])
                    i = sc_next()
                    for c2 in range(NCK):
                        tpm = b.op("te", lambda E, i=i, c2=c2, tau=tau: E.transpose(
                            SCb[i][:, c2 * 128:(c2 + 1) * 128],
                            pfm[:, c2 * T + tau * 128:c2 * T + (tau + 1) * 128],
                            identb[:]),
                            waits=(scfree[i] + [(b.ve, pev_done)]) if c2 == 0 else [],
                            mark=(c2 == NCK - 1))
                    ve_yt[tau] = b.op("ve", lambda E, i=i, v=v: E.tensor_tensor(
                        ys[:, v], SCb[i], xr[:, v], A_add),
                        waits=[(b.te, tpm), xrt, yst.get(tau - 2, None)], mark=True)
                    scfree[i] = [(b.ve, ve_yt[tau])]
                    yst[tau] = st_ys.put(s_y[sq, tau * 128:(tau + 1) * 128, :],
                                         ys[:, v], waits=[(b.ve, ve_yt[tau])])
                b.flush(blk)

        b.fence_all()
        with sbt("u1", [128, 32 * THALF], BF16) as u1, \
             sbt("g1", [128, 32 * THALF], BF16) as g1, \
             sbt("xn2", [128, NCK * THALF], BF16) as xn2:
            for hh in range(NHALF):
                th0 = hh * THALF
                b.fence_all()
                sc_reset()
                with sbt("yb", [128, 2, 1024], F32) as yb, \
                     sbt("xnst2", [128, 2, 1024], BF16) as xnst2, \
                     sbt("lnscr2", [128, 2, 16], F32) as lnscr2:
                    ve_x2 = {}
                    te_x2 = {}
                    b1_ev = {}
                    for tt in range(NTH):
                        v = tt % 2
                        yt = st_yb.put(yb[:, v],
                                       s_y[sq, th0 + tt * 128:th0 + (tt + 1) * 128, :],
                                       waits=[(b.ve, ve_x2.get(tt - 2, 0))])
                        m6 = ln_tile(yb[:, v], xnst2[:, v],
                                     (lnscr2[:, v, 0:12], lnscr2[:, v, 12:14],
                                      lnscr2[:, v, 14:16]),
                                     [yt, (b.te, te_x2.get(tt - 2, 0))])
                        tpm, vem = tm2fm2(xnst2[:, v], xn2, tt, [(b.ve, m6)])
                        te_x2[tt] = tpm
                        ve_x2[tt] = vem
                        b1_ev[tt] = vem
                    xn2_done = b.ve.n
                    b1_te = b.te.n
                    b.flush(blk)

                with sbt("w12s", [128, 2, 2, NCK * 256], BF16) as w12s, \
                     sbt("stmp", [128, 2, 1024], BF16) as stmp:
                    d_t = {}
                    te_m = {}
                    se_m = {}
                    ve_m = {}
                    for fs in range(16):
                        v = fs % 2
                        for wi, scr in ((0, s_w1), (1, s_w2)):
                            d_t[(fs, wi)] = st_w12.put(
                                w12s[:, v, wi], scr[fs],
                                waits=[(b.te, te_m.get((fs - 2, 1, 1), 0)),
                                       (b.te, b1_te), (b.ve, xn2_done)],
                                new_item=(wi == 0))
                        d_t[(fs, 0)] = d_t[(fs, 1)]
                        for sub in range(2):
                            fc = fs * 2 + sub
                            fv = fc % 2
                            for wi in range(2):
                                for ck in range(NCK):
                                    for tc in range(NTC2):
                                        mm = b.op("te", lambda E, fv=fv, wi=wi, v=v, sub=sub, ck=ck, tc=tc: E.matmul(
                                            PS[fv][:, wi * 1024 + tc * TCW2:wi * 1024 + (tc + 1) * TCW2],
                                            w12s[:, v, wi, ck * 256 + sub * 128:ck * 256 + sub * 128 + 128],
                                            xn2[:, ck * THALF + tc * TCW2:ck * THALF + (tc + 1) * TCW2],
                                            start=(ck == 0), stop=(ck == NCK - 1)),
                                            waits=[d_t[(fs, wi)],
                                                   (b.ve, b1_ev[min((tc + 1) * (TCW2 // 128), NTH) - 1]),
                                                   (b.se, se_m.get(fc - 2, 0)),
                                                   (b.ve, ve_m.get(fc - 2, 0))]
                                            if (ck == 0 and tc == 0) else [],
                                            mark=(ck == NCK - 1 and tc == NTC2 - 1))
                                te_m[(fs, sub, wi)] = mm
                            se_m[fc] = b.op("se", lambda E, fv=fv, fc=fc: E.activation(
                                stmp[:, fv, 0:THALF], PS[fv][:, 0:THALF], AF.Silu,
                                bias=zeroc[:], scale=s1c[:, fc:fc + 1]),
                                waits=[(b.te, te_m[(fs, sub, 0)])], mark=True)
                            ve_m[fc] = b.op("ve", lambda E, fv=fv, fc=fc: E.scalar_tensor_tensor(
                                u1[:, fc * THALF:(fc + 1) * THALF],
                                PS[fv][:, 1024:1024 + THALF], s2c[:, fc:fc + 1],
                                stmp[:, fv, 0:THALF], A_mult, A_mult),
                                waits=[(b.te, te_m[(fs, sub, 1)]),
                                       (b.se, se_m[fc])], mark=True)
                    u1_done = b.ve.n
                    b.flush(blk)

                def hstage(h_scr, src_sb, dst_sb, scvec, src_ready_ve, war_se):
                    b.fence_all()
                    with sbt("hsl", [128, 2, 32 * 256], BF16) as hsl:
                        d_t = {}
                        te_m = {}
                        se_m = {}
                        for fs in range(16):
                            v = fs % 2
                            d_t[fs] = st_hsl.put(hsl[:, v], h_scr[fs],
                                                 waits=[(b.te, te_m.get((fs - 2, 1), 0))])
                            for sub in range(2):
                                fc = fs * 2 + sub
                                fv = fc % 2
                                for ck in range(32):
                                    for tc in range(NTC2):
                                        mm = b.op("te", lambda E, fv=fv, v=v, sub=sub, ck=ck, tc=tc: E.matmul(
                                            PS[fv][:, tc * TCW2:(tc + 1) * TCW2],
                                            hsl[:, v, ck * 256 + sub * 128:ck * 256 + sub * 128 + 128],
                                            src_sb[:, ck * THALF + tc * TCW2:ck * THALF + (tc + 1) * TCW2],
                                            start=(ck == 0), stop=(ck == 31)),
                                            waits=[d_t[fs], (b.ve, src_ready_ve),
                                                   (b.se, se_m.get(fc - 2, 0)),
                                                   (b.se, war_se)]
                                            if (ck == 0 and tc == 0) else [],
                                            mark=(ck == 31 and tc == NTC2 - 1))
                                te_m[(fs, sub)] = mm
                                se_m[fc] = b.op("se", lambda E, fv=fv, fc=fc: E.activation(
                                    dst_sb[:, fc * THALF:(fc + 1) * THALF],
                                    PS[fv][:, 0:THALF], AF.Gelu,
                                    bias=zeroc[:], scale=scvec[:, fc:fc + 1]),
                                    waits=[(b.te, mm)], mark=True)
                        b.flush(blk)
                        return b.se.n, b.te.n

                g1_se, h1_te = hstage(s_h1, u1, g1, h1c, u1_done, 0)
                g2_se, h2_te = hstage(s_h2, g1, u1, h2c, 0, g1_se)
                g2 = u1

                b.fence_all()
                with sbt("w3s", [128, 32 * 256], BF16) as w3s, \
                     sbt("w3fm", [128, NCK * THALF], BF16) as w3fm, \
                     sbt("xr2", [128, 2, 1024], F32) as xr2, \
                     sbt("os_sb", [128, 2, 1024], F32) as os_sb:
                    te_m = {}
                    ve_m = {}
                    for ws in range(4):
                        d_t = st_w3.put(w3s[:], s_w3[ws],
                                        waits=[(b.te, te_m.get((ws - 1, 1), 0))])
                        for sub in range(2):
                            c2 = ws * 2 + sub
                            fv = c2 % 2
                            for ck in range(32):
                                for tc in range(NTC2):
                                    mm = b.op("te", lambda E, fv=fv, sub=sub, ck=ck, tc=tc: E.matmul(
                                        PS[fv][:, tc * TCW2:(tc + 1) * TCW2],
                                        w3s[:, ck * 256 + sub * 128:ck * 256 + sub * 128 + 128],
                                        g2[:, ck * THALF + tc * TCW2:ck * THALF + (tc + 1) * TCW2],
                                        start=(ck == 0), stop=(ck == 31)),
                                        waits=[d_t, (b.se, g2_se),
                                               (b.ve, ve_m.get(c2 - 2, 0))]
                                        if (ck == 0 and tc == 0) else [],
                                        mark=(ck == 31 and tc == NTC2 - 1))
                            te_m[(ws, sub)] = mm
                            ve_m[c2] = b.op("ve", lambda E, fv=fv, c2=c2: E.tensor_scalar(
                                w3fm[:, c2 * THALF:(c2 + 1) * THALF],
                                PS[fv][:, 0:THALF], w3c[:, c2:c2 + 1], None,
                                A_mult, A_byp),
                                waits=[(b.te, mm)], mark=True)
                    w3_done = b.ve.n
                    ve_o = {}
                    ost = {}
                    for tt in range(NTH):
                        v = tt % 2
                        xrt = st_xr2.put(xr2[:, v],
                                         s_y[sq, th0 + tt * 128:th0 + (tt + 1) * 128, :],
                                         waits=[(b.ve, ve_o.get(tt - 2, 0))])
                        for c2 in range(NCK):
                            tpm = b.op("te", lambda E, v=v, c2=c2, tt=tt: E.transpose(
                                PSb[v][:, c2 * 128:(c2 + 1) * 128],
                                w3fm[:, c2 * THALF + tt * 128:c2 * THALF + (tt + 1) * 128],
                                identb[:]),
                                waits=[(b.ve, w3_done), (b.ve, ve_o.get(tt - 2, 0))]
                                if c2 == 0 else [],
                                mark=(c2 == NCK - 1))
                        ve_o[tt] = b.op("ve", lambda E, v=v: E.tensor_tensor(
                            os_sb[:, v], PSb[v][:, 0:1024], xr2[:, v], A_add),
                            waits=[(b.te, tpm), xrt, ost.get(tt - 2, None)], mark=True)
                        ost[tt] = st_out.put(out[sq, th0 + tt * 128:th0 + (tt + 1) * 128, :],
                                             os_sb[:, v], waits=[(b.ve, ve_o[tt])])
                        outst.append(ost[tt])
                    b.flush(blk)

    for (s, v) in outst:
        b.wait("gp", s, v)
    b.fence_all()
    b.flush(blk)
    blk.__exit__(None, None, None)
    nc.finalize()
    return nc


def make_consts():
    import ml_dtypes
    ident_f = np.eye(128, dtype=np.float32)
    ident_b = ident_f.astype(ml_dtypes.bfloat16)
    t = np.arange(128)
    cmask = np.where(t[:, None] <= t[None, :], 0.0, -1e9).astype(np.float32)
    return ident_b, ident_f, cmask


NAME_MAP = {
    "Wq": "L0_attn_Wq", "Wk": "L0_attn_Wk", "Wv": "L0_attn_Wv",
    "Aq": "L0_attn_Aq", "Ak": "L0_attn_Ak", "Av": "L0_attn_Av",
    "Bq": "L0_attn_Bq", "Bk": "L0_attn_Bk", "Bv": "L0_attn_Bv",
    "Wp": "L0_attn_Wp", "bp": "L0_attn_bp", "Ap": "L0_attn_Ap",
    "Bp": "L0_attn_Bp",
    "w1": "L0_ffn_w1_w", "s1": "L0_ffn_w1_s",
    "w2": "L0_ffn_w2_w", "s2": "L0_ffn_w2_s",
    "h1": "L0_ffn_h1_w", "hs1": "L0_ffn_h1_s",
    "h2": "L0_ffn_h2_w", "hs2": "L0_ffn_h2_s",
    "w3": "L0_ffn_w3_w", "s3": "L0_ffn_w3_s",
}


def kernel(**inputs):
    n_cores = 8
    x = np.ascontiguousarray(inputs["L0_attn_x"], dtype=np.float32)
    ident_b, ident_f, cmask = make_consts()
    base = {k: np.ascontiguousarray(inputs[v], dtype=np.float32)
            for k, v in NAME_MAP.items()}
    base["identb"] = ident_b
    base["identf"] = ident_f
    base["cmask"] = cmask
    n_seq = x.shape[0] // n_cores
    nc = build_core_program(n_seq=n_seq, T=x.shape[1])
    in_maps = []
    for i in range(n_cores):
        m = dict(base)
        m["x"] = np.ascontiguousarray(x[i * n_seq:(i + 1) * n_seq])
        in_maps.append(m)
    res = run_bass_kernel_spmd(nc, in_maps, core_ids=list(range(n_cores)))
    return np.concatenate([r["out"] for r in res.results], axis=0)
